# version 48
# baseline (speedup 1.0000x reference)
import numpy as np
import ml_dtypes
from contextlib import ExitStack

import concourse.bass as bass
import concourse.mybir as mybir
from concourse.bass_utils import run_bass_kernel_spmd

F32 = mybir.dt.float32
BF16 = mybir.dt.bfloat16
I32 = mybir.dt.int32
I8 = mybir.dt.int8
AF = mybir.ActivationFunctionType
OP = mybir.AluOpType

D = 1024
M = 64
H = 3072
N = 2048
CH = 128
NCH = N // CH
NDC = D // 128
NHC = H // 128
TL = 256
QS = float(np.float32(1.0 / 127.0))
EPS = 1e-5
DECAY_MIN, DECAY_MAX = 0.9, 0.999


class Sem:
    def __init__(self, handle):
        self.sem = handle
        self.n = 0


class Eng:

    def __init__(self, ops):
        self.ops = ops

    def wait(self, ctr, val):
        if val <= 0:
            return
        self.ops.append(lambda e, s=ctr.sem, v=val: e.wait_ge(s, v))

    def op(self, fn, inc=None, k=1):
        if inc is None:
            self.ops.append(fn)
            return None
        inc.n += k
        self.ops.append(lambda e, f=fn, s=inc.sem, kk=k: f(e).then_inc(s, kk))
        return inc.n

    def drain(self):
        self.ops.append(lambda e: e.drain())


class View:

    def __init__(self, base, c0):
        self.base = base
        self.c0 = c0

    def __getitem__(self, idx):
        rows, cols = idx
        if isinstance(cols, slice):
            cols = slice(self.c0 + (cols.start or 0), self.c0 + cols.stop)
        return self.base[rows, cols]


def build_nc():
    nc = bass.Bass("TRN2", target_bir_lowering=False)

    xfm = nc.dram_tensor("xfm", [8, 128, NDC * TL], F32, kind="ExternalInput").ap()
    xpfm = nc.dram_tensor("xpfm", [8, 128, NDC * TL], F32, kind="ExternalInput").ap()
    xtm = nc.dram_tensor("xtm", [N, D], F32, kind="ExternalInput").ap()
    wqk = nc.dram_tensor("wqk", [128, NDC * 128], F32, kind="ExternalInput").ap()
    wkvg = nc.dram_tensor("wkvg", [128, NDC * 160], F32, kind="ExternalInput").ap()
    wob = nc.dram_tensor("wob", [M + 1, D], F32, kind="ExternalInput").ap()
    wgu = nc.dram_tensor("wgu", [NHC, 128, 2048], I8, kind="ExternalInput").ap()
    wdt = nc.dram_tensor("wdt", [NHC, 128, 1024], I8, kind="ExternalInput").ap()
    dec = nc.dram_tensor("dec", [1, 1], F32, kind="ExternalInput").ap()
    bgs = nc.dram_tensor("bgs", [1, 1], F32, kind="ExternalInput").ap()
    out = nc.dram_tensor("out", [N, D], F32, kind="ExternalOutput").ap()

    with ExitStack() as stack:
        E_ = stack.enter_context

        xn = E_(nc.sbuf_tensor("xn", [128, NDC * N], BF16))
        wd_bf = E_(nc.sbuf_tensor("wd_bf", [128, NHC * 1024], BF16))
        h_bf = E_(nc.sbuf_tensor("h_bf", [128, NHC * 1024], BF16))
        xstage = [E_(nc.sbuf_tensor(f"xstage{i}", [128, NDC * TL], BF16)) for i in range(3)]
        xsq_f = [E_(nc.sbuf_tensor(f"xsqf{i}", [128, NDC * TL], BF16)) for i in range(2)]
        m_sb = [E_(nc.sbuf_tensor(f"msb{i}", [128, TL], F32)) for i in range(2)]
        gu_bf = [E_(nc.sbuf_tensor(f"gubf{i}", [128, 2048], BF16)) for i in range(3)]
        qt_bf = E_(nc.sbuf_tensor("qt_bf", [M, N], BF16))
        k_bf = E_(nc.sbuf_tensor("k_bf", [M, N], BF16))
        kv_bf = E_(nc.sbuf_tensor("kv_bf", [128, NCH * 128], BF16))
        outsa = E_(nc.sbuf_tensor("outsa", [M + 1, N], BF16))
        zcol = [E_(nc.sbuf_tensor(f"zcol{i}", [128, NCH], F32)) for i in range(2)]
        gcol = [E_(nc.sbuf_tensor(f"gcol{i}", [128, NCH], F32)) for i in range(2)]
        gwcol = [E_(nc.sbuf_tensor(f"gwcol{i}", [128, NCH], F32)) for i in range(2)]
        gdcol = E_(nc.sbuf_tensor("gdcol", [128, NCH], F32))
        vtmp = [E_(nc.sbuf_tensor(f"vtmp{i}", [128, TL], F32)) for i in range(2)]
        rstd = [E_(nc.sbuf_tensor(f"rstd{i}", [128, TL], F32)) for i in range(2)]
        abig = E_(nc.sbuf_tensor("abig", [128, NDC * TL // 2], F32))
        wqk_bf = E_(nc.sbuf_tensor("wqk_bf", [128, NDC * 128], BF16))
        wkvg_bf = E_(nc.sbuf_tensor("wkvg_bf", [128, NDC * 160], BF16))
        wob_bf = E_(nc.sbuf_tensor("wob_bf", [M + 1, D], BF16))
        xtmt = [E_(nc.sbuf_tensor(f"xtmt{i}", [128, D], F32)) for i in range(2)]
        gf_sb = abig
        sg_bf = E_(nc.sbuf_tensor("sgbf", [128, D], BF16))
        pt_bf = [E_(nc.sbuf_tensor(f"ptbf{i}", [128, 128], BF16)) for i in range(2)]
        tril_bf = E_(nc.sbuf_tensor("tril_bf", [128, 128], BF16))
        c1024 = E_(nc.sbuf_tensor("c1024", [128, 128], BF16))
        eps_c = E_(nc.sbuf_tensor("eps_c", [128, 1], F32))
        dec_s = E_(nc.sbuf_tensor("dec_s", [1, 1], F32))
        bgs_s = E_(nc.sbuf_tensor("bgs_s", [1, 1], F32))
        sig_s = E_(nc.sbuf_tensor("sig_s", [1, 1], F32))
        dcy_s = E_(nc.sbuf_tensor("dcy_s", [1, 1], F32))
        tscr = E_(nc.sbuf_tensor("tscr", [1, 1], F32))
        ld_s = E_(nc.sbuf_tensor("ld_s", [1, 1], F32))
        ones_r = E_(nc.sbuf_tensor("ones_r", [1, 128], F32))
        ldbc = E_(nc.sbuf_tensor("ldbc", [128, 1], F32))
        bgbc = E_(nc.sbuf_tensor("bgbc", [128, 1], F32))
        w64 = E_(nc.sbuf_tensor("w64", [M, 128], F32))
        wr_col = E_(nc.sbuf_tensor("wr_col", [128, 1], F32))
        dneg_col = E_(nc.sbuf_tensor("dneg_col", [128, 1], F32))
        dC_col = E_(nc.sbuf_tensor("dC_col", [128, 1], F32))
        S_bf = [E_(nc.sbuf_tensor(f"Sbf{i}", [M, M], BF16)) for i in range(2)]
        wrp_ext = E_(nc.sbuf_tensor("wrp_ext", [128, NCH], F32))

        xnp_w = [View(wd_bf, 20 * 1024), View(wd_bf, 22 * 1024)]

        pb = [E_(nc.psum_tensor(f"pb{i}", [128, 1024], F32)) for i in range(4)]

        mk = lambda name: Sem(E_(nc.semaphore(name)))
        dSM = mk("dSM")
        dSM2 = mk("dSM2")
        dX = [mk("dXA"), mk("dXB"), mk("dXC")]
        dX2 = [mk("dXA2"), mk("dXB2"), mk("dXC2")]
        dGU = [mk("dGU0"), mk("dGU1"), mk("dGU2")]
        dGUg = [mk("dGUg0"), mk("dGUg1"), mk("dGUg2")]
        dWD = mk("dWD")
        dXT = [mk("dXT0"), mk("dXT1")]
        dOT = [mk("dOT0"), mk("dOT1")]
        T = mk("T"); V = mk("V"); S = mk("S"); G = mk("G")

        ops = {k: [] for k in "tvsgy"}
        t = Eng(ops["t"]); v = Eng(ops["v"]); s = Eng(ops["s"])
        g = Eng(ops["g"]); y = Eng(ops["y"])

        st8 = {}

        cur_tset = ["setup"]
        dummy_last = [0]

        def s_pre(fname):
            if cur_tset[0] != fname:
                cur_tset[0] = fname
                s.wait(S, dummy_last[0])
                dummy_last[0] = s.op(lambda e, f=fname: e.activation(
                    out=tscr[:], in_=tscr[:], func=getattr(AF, f)), S, 1)

        for dst, src in [(dec_s[:], dec), (bgs_s[:], bgs)]:
            s.op(lambda e, d_=dst, s_=src: e.dma_start(out=d_, in_=s_), dSM, 16)
        scalars_done = dSM.n
        s.op(lambda e: e.dma_start(out=abig[:], in_=wqk), dSM, 16)
        smalls_done = dSM.n
        for dst, src in [(xtmt[0][:], wkvg[:, 0:1024]),
                         (xtmt[1][:, 0:256], wkvg[:, 1024:1280])]:
            s.op(lambda e, d_=dst, s_=src: e.dma_start(out=d_, in_=s_), dSM2, 16)
        kvg_done = dSM2.n

        g.op(lambda e: e.iota(out=wr_col[:], pattern=[[1, 1]], base=127,
                              channel_multiplier=-1,
                              allow_small_or_imprecise_dtypes=True))
        g.op(lambda e: e.iota(out=dneg_col[:], pattern=[[1, 1]], base=-1,
                              channel_multiplier=-1,
                              allow_small_or_imprecise_dtypes=True))
        g.op(lambda e: e.iota(out=w64[:], pattern=[[1, 128]], base=1,
                              channel_multiplier=0,
                              allow_small_or_imprecise_dtypes=True))
        g.op(lambda e: e.iota(out=wrp_ext[:], pattern=[[-128, NCH]], base=2047,
                              channel_multiplier=-1,
                              allow_small_or_imprecise_dtypes=True))
        g.op(lambda e: e.memset(tril_bf[:], 1.0))
        g.drain()
        g_setup = g.op(lambda e: e.affine_select(
            out=tril_bf[:], in_=tril_bf[:], pattern=[[1, 128]],
            compare_op=OP.is_ge, fill=0.0, base=0, channel_multiplier=-1), G, 1)

        v.op(lambda e: e.memset(eps_c[:], EPS))
        v.op(lambda e: e.memset(c1024[:], 1.0 / 1024.0))
        v.op(lambda e: e.memset(vtmp[0][0:1, 0:128], 1.0))
        v.op(lambda e: e.memset(ones_r[:], 1.0))
        v.op(lambda e: e.memset(outsa[M:M + 1, :], 1.0))
        v.op(lambda e: e.memset(S_bf[0][:], 0.0), V, 1)
        v_const = V.n
        v.wait(dSM, smalls_done)
        v_wqk = v.op(lambda e: e.tensor_copy(out=wqk_bf[:], in_=abig[:]), V, 1)
        v.wait(dSM2, kvg_done)
        v.op(lambda e: e.tensor_copy(out=wkvg_bf[:, 0:1024], in_=xtmt[0][:]))
        v_w1 = v.op(lambda e: e.tensor_copy(out=wkvg_bf[:, 1024:1280],
                                            in_=xtmt[1][:, 0:256]), V, 1)
        s.wait(V, v_wqk)
        s.op(lambda e: e.dma_start(out=abig[0:M + 1, :], in_=wob), dSM, 16)
        smalls2 = dSM.n
        v.wait(dSM, smalls2)
        v.op(lambda e: e.tensor_copy(out=wob_bf[:], in_=abig[0:M + 1, :]), V, 1)
        v_wsmall = V.n

        s.wait(V, v_const)
        dummy_last[0] = s.op(lambda e: e.activation(
            out=tscr[:], in_=eps_c[0:1, 0:1], func=AF.Square), S, 1)

        def decay_setup():
            s.wait(dSM, scalars_done)
            s_sig = s.op(lambda e: e.activation(out=sig_s[:], in_=dec_s[:],
                                                func=AF.Sigmoid), S, 1)
            v.wait(S, s_sig)
            v_dcy = v.op(lambda e: e.tensor_scalar(out=dcy_s[:], in0=sig_s[:],
                                                   scalar1=DECAY_MAX - DECAY_MIN,
                                                   scalar2=DECAY_MIN, op0=OP.mult,
                                                   op1=OP.add), V, 1)
            s.wait(V, v_dcy)
            s_ld = s.op(lambda e: e.activation(out=ld_s[:], in_=dcy_s[:],
                                               func=AF.Ln), S, 1)
            t.wait(S, s_ld)
            t.wait(V, v_const)
            t.op(lambda e: e.matmul(out=pb[3][:, 448:449], lhsT=ones_r[:],
                                    rhs=ld_s[:], start=True, stop=True))
            t_bcast = t.op(lambda e: e.matmul(out=pb[3][:, 960:961],
                                              lhsT=ones_r[:],
                                              rhs=bgs_s[:], start=True, stop=True),
                           T, 1)
            v.wait(T, t_bcast)
            v.wait(G, g_setup)
            v.op(lambda e: e.tensor_copy(out=ldbc[:], in_=pb[3][:, 448:449]))
            v.op(lambda e: e.tensor_copy(out=bgbc[:], in_=pb[3][:, 960:961]))
            v.drain()
            v.op(lambda e: e.tensor_scalar(out=w64[:], in0=w64[:],
                                           scalar1=ldbc[0:M, :], scalar2=None,
                                           op0=OP.mult))
            v.op(lambda e: e.tensor_scalar(out=wr_col[:], in0=wr_col[:],
                                           scalar1=ldbc[:], scalar2=None,
                                           op0=OP.mult))
            v.op(lambda e: e.tensor_scalar(out=wrp_ext[:], in0=wrp_ext[:],
                                           scalar1=ldbc[:], scalar2=None,
                                           op0=OP.mult))
            v.op(lambda e: e.tensor_scalar(out=dneg_col[:], in0=dneg_col[:],
                                           scalar1=ldbc[:], scalar2=None,
                                           op0=OP.mult))
            v_preexp = v.op(lambda e: e.tensor_scalar(out=dC_col[:], in0=ldbc[:],
                                                      scalar1=128.0, scalar2=None,
                                                      op0=OP.mult), V, 1)
            s.wait(V, v_preexp)
            s.op(lambda e: e.activation(out=w64[:], in_=w64[:], func=AF.Exp))
            s.op(lambda e: e.activation(out=wr_col[:], in_=wr_col[:], func=AF.Exp))
            s.op(lambda e: e.activation(out=wrp_ext[:], in_=wrp_ext[:], func=AF.Exp))
            s.op(lambda e: e.activation(out=dneg_col[:], in_=dneg_col[:],
                                        func=AF.Exp))
            s_decay = s.op(lambda e: e.activation(out=dC_col[:], in_=dC_col[:],
                                                  func=AF.Exp), S, 1)
            st8["s_decay"] = s_decay
            st8["v_preexp"] = v_preexp

        apply_done = [0] * 16
        ln_mm_done = [0] * 16
        x_issue = [0] * 16
        x_issue2 = [0] * 16
        kvgp_evac = [0] * NCH
        kvgp_mm = [0] * NCH
        kvgo_evac = [0] * NCH
        qk_evac = [0] * 8
        t_qk = [0] * 8
        v_ba_p = [0] * NCH
        v_ba = [0] * NCH
        v_pt = [0] * NCH
        v_sdone = [0] * NCH
        t_p = [0] * NCH
        wd_pending = list(range(20))
        sparholder = [0]

        def emit_wd_loads(k):
            for _ in range(k):
                if not wd_pending:
                    return
                hc = wd_pending.pop(0)
                g.op(lambda e, hc=hc:
                     e.dma_start(out=wd_bf[:, hc * 1024:(hc + 1) * 1024],
                                 in_=wdt[hc]), dWD, 16)

        def m_ps_of(par):
            return pb[2][:, par * 512: par * 512 + TL]

        def s2_ps_of(par):
            return pb[3][:, par * 512: par * 512 + TL]

        sq_done = [0] * 16

        def lnA_p(ti):
            p3 = ti % 3
            own = ti < 8
            tt = ti if own else ti - 8
            if ti >= 3:
                g.wait(V, apply_done[ti - 3])
            src = (xfm if own else xpfm)[tt]
            g.op(lambda e, d_=xstage[p3][:, 0:1024], s_=src[:, 0:1024]:
                 e.dma_start(out=d_, in_=s_), dX[p3], 16)
            x_issue[ti] = dX[p3].n
            g.op(lambda e, d_=xstage[p3][:, 1024:2048], s_=src[:, 1024:2048]:
                 e.dma_start(out=d_, in_=s_), dX2[p3], 16)
            x_issue2[ti] = dX2[p3].n
            s.wait(dX[p3], x_issue[ti])
            if ti >= 2:
                s.wait(T, ln_mm_done[ti - 2])
            sq_h = [0, 0]
            for jh in range(2):
                if jh == 1:
                    s.wait(dX2[p3], x_issue2[ti])
                sq_h[jh] = s.op(
                    lambda e, jh=jh, a=xsq_f[ti % 2], b=xstage[p3]:
                    e.activation(out=a[:, jh * 1024:(jh + 1) * 1024],
                                 in_=b[:, jh * 1024:(jh + 1) * 1024],
                                 func=AF.Square), S, 1)
            sq_done[ti] = tuple(sq_h)

        def lnA_c(ti):
            par = ti % 2
            p3 = ti % 3
            m_ps, s2_ps = m_ps_of(par), s2_ps_of(par)
            t.wait(dX[p3], x_issue[ti])
            if ti >= 2:
                t.wait(V, st8[("v_var", ti - 2)])
            if ti >= 10:
                t.wait(V, kvgp_evac[2 * (ti - 2 - 8) + 1])
            for dc in range(NDC):
                if dc == NDC // 2:
                    t.wait(dX2[p3], x_issue2[ti])
                t.op(lambda e, o=m_ps, a=xstage[p3][:, dc * TL:(dc + 1) * TL],
                     st=(dc == 0), sp=(dc == NDC - 1):
                     e.matmul(out=o, lhsT=c1024[:], rhs=a, start=st, stop=sp))
            t.wait(S, sq_done[ti][0])
            for dc in range(NDC):
                if dc == NDC // 2:
                    t.wait(S, sq_done[ti][1])
                last = (lambda e, o=s2_ps,
                        a=xsq_f[ti % 2][:, dc * TL:(dc + 1) * TL],
                        st=(dc == 0), sp=(dc == NDC - 1):
                        e.matmul(out=o, lhsT=c1024[:], rhs=a, start=st, stop=sp))
                if dc == NDC - 1:
                    ln_mm_done[ti] = t.op(last, T, 1)
                else:
                    t.op(last)

        aph = {}

        def lnB_p(ti):
            par = ti % 2
            p3 = ti % 3
            own = ti < 8
            tt = ti if own else ti - 8
            m_ps, s2_ps = m_ps_of(par), s2_ps_of(par)
            s.wait(T, ln_mm_done[ti])
            if ti - 2 >= 8:
                s.wait(T, kvgp_mm[2 * (ti - 2 - 8) + 1])
            s_msq = s.op(lambda e, o=vtmp[par], a=m_ps:
                         e.activation(out=o[:], in_=a, func=AF.Square), S, 1)
            s_pre("Sqrt")
            v.wait(S, s_msq)
            v.op(lambda e, o=m_sb[par], a=m_ps:
                 e.tensor_copy(out=o[:], in_=a))
            v_var = v.op(lambda e, o=vtmp[par], a=s2_ps:
                         e.tensor_tensor(out=o[:], in0=a, in1=o[:],
                                         op=OP.subtract), V, 1)
            st8[("v_var", ti)] = v_var
            s.wait(V, v_var)
            s_std = s.op(lambda e, o=vtmp[par]:
                         e.activation(out=o[:], in_=o[:], func=AF.Sqrt,
                                      bias=eps_c[:]), S, 1)
            v.wait(S, s_std)
            v.op(lambda e, o=rstd[par], a=vtmp[par]:
                 e.reciprocal(out=o[:], in_=a[:]), V, 1)
            if not own and ti >= 10:
                v.wait(T, kvgp_mm[2 * (ti - 2 - 8) + 1])
            if ti >= 1:
                v.wait(V, apply_done[ti - 1])
            else:
                v.wait(V, v_wsmall)
            last = 0
            ap_half = [0, 0]
            for j in range(2):
                xin3 = bass.AP(xstage[p3], j * 128,
                               [[NDC * TL, 128], [TL, NDC], [1, 128]])
                ab3 = bass.AP(abig, 0, [[NDC * 128, 128], [128, NDC], [1, 128]])
                m_bc = bass.AP(m_sb[par], j * 128,
                               [[TL, 128], [0, NDC], [1, 128]])
                r_bc = bass.AP(rstd[par], j * 128,
                               [[TL, 128], [0, NDC], [1, 128]])
                if own:
                    dst3 = bass.AP(xn, tt * TL + j * 128,
                                   [[NDC * N, 128], [N, NDC], [1, 128]])
                else:
                    c0 = (20 + 2 * par) * 1024 + j * 128
                    dst3 = bass.AP(wd_bf, c0,
                                   [[NHC * 1024, 128], [TL, NDC], [1, 128]])
                if j == 1:
                    v.wait(V, last)
                v_sub = v.op(lambda e, o=ab3, a=xin3, b=m_bc:
                             e.tensor_tensor(out=o, in0=a, in1=b,
                                             op=OP.subtract), V, 1)
                v.wait(V, v_sub)
                last = v.op(lambda e, o=dst3, a=ab3, b=r_bc:
                            e.tensor_tensor(out=o, in0=a, in1=b, op=OP.mult),
                            V, 1)
                ap_half[j] = last
            aph[ti] = ap_half
            apply_done[ti] = last

        def lnB_c(ti):
            par = ti % 2
            own = ti < 8
            tt = ti if own else ti - 8
            ap_half = aph[ti]
            if not own:
                for j in range(2):
                    cc = 2 * tt + j
                    kvg_ps = (pb[2] if j == 0 else pb[3])[
                        :, par * 512 + 256: par * 512 + 256 + 129]
                    t.wait(V, ap_half[j])
                    t.wait(V, v_wsmall)
                    if cc >= 2:
                        t.wait(V, kvgp_evac[cc - 2])
                    t_mm = None
                    for dc in range(NDC):
                        sp = dc == NDC - 1
                        fn = (lambda e, o=kvg_ps, dc=dc, par=par, j=j,
                              st=(dc == 0), sp=sp:
                              e.matmul(out=o,
                                       lhsT=xnp_w[par][:, dc * TL + j * 128: dc * TL + j * 128 + 128],
                                       rhs=wkvg_bf[:, dc * 160: dc * 160 + 129],
                                       start=st, stop=sp))
                        if sp:
                            t_mm = t.op(fn, T, 1)
                        else:
                            t.op(fn)
                    kvgp_mm[cc] = t_mm
                    v.wait(T, t_mm)
                    v.op(lambda e, o=kv_bf[:, cc * 128:(cc + 1) * 128],
                         a=kvg_ps: e.tensor_copy(out=o, in_=a[:, 0:128]))
                    kvgp_evac[cc] = v.op(
                        lambda e, o=zcol[0][:, cc:cc + 1], a=kvg_ps:
                        e.tensor_copy(out=o, in_=a[:, 128:129]), V, 1)
                if ti == 15:
                    g.wait(T, kvgp_mm[NCH - 1])
                    for hc in (20, 21, 22, 23):
                        g.op(lambda e, hc=hc:
                             e.dma_start(out=wd_bf[:, hc * 1024:(hc + 1) * 1024],
                                         in_=wdt[hc]), dWD, 16)

        def sum_prep():
            s_pre("Sigmoid")
            s.wait(V, kvgp_evac[NCH - 1])
            s.wait(S, st8["s_decay"])
            s_gprev = s.op(lambda e: e.activation(out=gcol[0][:], in_=zcol[0][:],
                                                  func=AF.Sigmoid, bias=bgbc[:]),
                           S, 1)
            v.wait(S, s_gprev)
            v_gwp = v.op(lambda e: e.tensor_tensor(out=gwcol[0][:],
                                                   in0=gcol[0][:],
                                                   in1=wrp_ext[:], op=OP.mult),
                         V, 1)
            v.wait(V, v_gwp)
            v.wait(V, kvgp_evac[NCH - 1])
            for cc in range(NCH):
                v.op(lambda e, cc=cc:
                     e.tensor_scalar(out=kv_bf[:, cc * 128 + 64: cc * 128 + 128],
                                     in0=kv_bf[:, cc * 128 + 64: cc * 128 + 128],
                                     scalar1=gcol[0][:, cc:cc + 1], scalar2=None,
                                     op0=OP.mult))
                v_ba_p[cc] = v.op(lambda e, cc=cc:
                                  e.tensor_scalar(
                                      out=kv_bf[:, cc * 128: cc * 128 + 64],
                                      in0=kv_bf[:, cc * 128: cc * 128 + 64],
                                      scalar1=gwcol[0][:, cc:cc + 1],
                                      scalar2=None, op0=OP.mult), V, 1)

        def sum_chunks():
            ds_ps = pb[2][0:M, 384:448]
            t.wait(V, v_ba_p[NCH - 1])
            t.wait(V, apply_done[15])
            t_mm = None
            for cc in range(NCH):
                sp = cc == NCH - 1
                fn = (lambda e, o=ds_ps, cc=cc, st=(cc == 0), sp=sp:
                      e.matmul(out=o,
                               lhsT=kv_bf[:, cc * 128: cc * 128 + 64],
                               rhs=kv_bf[:, cc * 128 + 64: cc * 128 + 128],
                               start=st, stop=sp))
                if sp:
                    t_mm = t.op(fn, T, 1)
                else:
                    t.op(fn)
            v.wait(T, t_mm)
            st8["v_sbf_init"] = v.op(lambda e, a=ds_ps:
                                     e.tensor_copy(out=S_bf[0][:], in_=a), V, 1)

        def qk_pass(idx):
            nt, which = idx // 2, idx % 2
            npar = idx % 2
            qk_ps = pb[3][:, npar * 512:(npar + 1) * 512]
            t.wait(V, apply_done[7])
            if idx >= 2:
                t.wait(V, qk_evac[idx - 2])
            else:
                t.wait(V, apply_done[15])
                t.wait(V, kvgp_evac[NCH - 1])
            for dc in range(NDC):
                sp = dc == NDC - 1
                fn = (lambda e, o=qk_ps, dc=dc, nt=nt, w=which,
                      st=(dc == 0), sp=sp:
                      e.matmul(out=o[0:M, :],
                               lhsT=wqk_bf[:, dc * 128 + w * 64: dc * 128 + w * 64 + 64],
                               rhs=xn[:, dc * N + nt * 512: dc * N + (nt + 1) * 512],
                               start=st, stop=sp))
                if sp:
                    t_qk[idx] = t.op(fn, T, 1)
                else:
                    t.op(fn)
            v.wait(T, t_qk[idx])
            if which == 0:
                w64b = bass.AP(w64, 0, [[128, M], [0, 4], [1, 128]])
                qo = bass.AP(qt_bf, nt * 512, [[N, M], [128, 4], [1, 128]])
                qk_evac[idx] = v.op(
                    lambda e, o=qo, a=qk_ps, b=w64b:
                    e.tensor_tensor(out=o,
                                    in0=bass.AP(a.tensor, a.offset,
                                                [[1024, M], [128, 4], [1, 128]]),
                                    in1=b, op=OP.mult), V, 1)
            else:
                qk_evac[idx] = v.op(
                    lambda e, o=k_bf[:, nt * 512:(nt + 1) * 512], a=qk_ps:
                    e.tensor_copy(out=o, in_=a[0:M, :]), V, 1)

        def d2a_chunk(cc):
            kvg_ps = (pb[2] if cc % 2 == 0 else pb[3])[
                :, ((cc // 2) % 2) * 512 + 256: ((cc // 2) % 2) * 512 + 256 + 129]
            t.wait(V, kvgo_evac[cc - 2] if cc >= 2 else st8["v_sbf_init"])
            if cc < 2:
                t.wait(V, qk_evac[7])
            t_mm = None
            for dc in range(NDC):
                sp = dc == NDC - 1
                fn = (lambda e, o=kvg_ps, dc=dc, cc=cc, st=(dc == 0), sp=sp:
                      e.matmul(out=o,
                               lhsT=xn[:, dc * N + cc * 128: dc * N + (cc + 1) * 128],
                               rhs=wkvg_bf[:, dc * 160: dc * 160 + 129],
                               start=st, stop=sp))
                if sp:
                    t_mm = t.op(fn, T, 1)
                else:
                    t.op(fn)
            v.wait(T, t_mm)
            v.op(lambda e, o=kv_bf[:, cc * 128:(cc + 1) * 128], a=kvg_ps:
                 e.tensor_copy(out=o, in_=a[:, 0:128]))
            kvgo_evac[cc] = v.op(lambda e, o=zcol[1][:, cc:cc + 1], a=kvg_ps:
                                 e.tensor_copy(out=o, in_=a[:, 128:129]), V, 1)

        def d2_gates():
            s_pre("Sigmoid")
            s.wait(V, kvgo_evac[NCH - 1])
            s_gown = s.op(lambda e: e.activation(out=gcol[1][:], in_=zcol[1][:],
                                                 func=AF.Sigmoid, bias=bgbc[:]),
                          S, 1)
            v.wait(S, s_gown)
            v.op(lambda e: e.tensor_scalar(out=gwcol[1][:], in0=gcol[1][:],
                                           scalar1=wr_col[:], scalar2=None,
                                           op0=OP.mult))
            v_gw1 = v.op(lambda e: e.tensor_scalar(out=gdcol[:], in0=gcol[1][:],
                                                   scalar1=dneg_col[:],
                                                   scalar2=None, op0=OP.mult),
                         V, 1)
            v.wait(V, v_gw1)
            v.wait(V, kvgo_evac[NCH - 1])

        def d2_ba():
            for cc in range(NCH):
                v.op(lambda e, cc=cc:
                     e.tensor_scalar(out=kv_bf[:, cc * 128 + 64: cc * 128 + 128],
                                     in0=kv_bf[:, cc * 128 + 64: cc * 128 + 128],
                                     scalar1=gcol[1][:, cc:cc + 1], scalar2=None,
                                     op0=OP.mult))
                v_ba[cc] = v.op(lambda e, cc=cc:
                                e.tensor_scalar(
                                    out=kv_bf[:, cc * 128: cc * 128 + 64],
                                    in0=kv_bf[:, cc * 128: cc * 128 + 64],
                                    scalar1=gwcol[1][:, cc:cc + 1],
                                    scalar2=None, op0=OP.mult), V, 1)

        def d2b_a(cc):
            cpar = cc % 2
            p_ps = pb[3][:, cpar * 512: cpar * 512 + 128]
            t.wait(V, v_ba[cc])
            t.wait(V, v_pt[cc - 2] if cc >= 2 else qk_evac[7])
            t_p[cc] = t.op(lambda e, o=p_ps, cc=cc:
                           e.matmul(out=o, lhsT=k_bf[:, cc * 128:(cc + 1) * 128],
                                    rhs=qt_bf[:, cc * 128:(cc + 1) * 128],
                                    start=True, stop=True), T, 1)

        def d2b_b(cc):
            cpar = cc % 2
            p_ps = pb[3][:, cpar * 512: cpar * 512 + 128]
            v.wait(T, t_p[cc])
            v_pt[cc] = v.op(lambda e, o=pt_bf[cpar], a=p_ps, cc=cc:
                            e.scalar_tensor_tensor(out=o[:], in0=a,
                                                   scalar=gdcol[:, cc:cc + 1],
                                                   in1=tril_bf[:], op0=OP.mult,
                                                   op1=OP.mult), V, 1)

        def d2b_c(cc):
            cpar = cc % 2
            sparity = cc % 2
            o_ps = pb[2][0:M, cpar * 512: cpar * 512 + 128]
            ds_ps = pb[2][0:M, cpar * 512 + 384: cpar * 512 + 448]
            t.wait(V, v_pt[cc])
            t.wait(V, v_sdone[cc - 1] if cc >= 1 else st8["v_sbf_init"])
            if cc >= 2:
                t.wait(V, v_sdone[cc - 2])
            t.op(lambda e, o=o_ps, cc=cc, cpar=cpar:
                 e.matmul(out=o, lhsT=kv_bf[:, cc * 128 + 64: cc * 128 + 128],
                          rhs=pt_bf[cpar][:], start=True, stop=False))
            t_o = t.op(lambda e, o=o_ps, cc=cc, sp=sparity:
                       e.matmul(out=o, lhsT=S_bf[sp][:],
                                rhs=qt_bf[:, cc * 128:(cc + 1) * 128],
                                start=False, stop=True), T, 1)
            t_ds2 = t.op(lambda e, o=ds_ps, cc=cc:
                         e.matmul(out=o,
                                  lhsT=kv_bf[:, cc * 128: cc * 128 + 64],
                                  rhs=kv_bf[:, cc * 128 + 64: cc * 128 + 128],
                                  start=True, stop=True), T, 1)
            v.wait(T, t_ds2)
            if cc == 0:
                v.wait(V, st8["v_sbf_init"])
            v.op(lambda e, o=outsa[0:M, cc * 128:(cc + 1) * 128], a=o_ps:
                 e.tensor_copy(out=o, in_=a))
            v_sdone[cc] = v.op(
                lambda e, o=S_bf[1 - sparity], i0=S_bf[sparity], ds=ds_ps:
                e.scalar_tensor_tensor(out=o[:], in0=i0[:],
                                       scalar=dC_col[0:M, :], in1=ds,
                                       op0=OP.mult, op1=OP.add), V, 1)

        pre = []
        for k in range(19):
            if k - 3 >= 0 and k - 3 < 16:
                pre.append(lambda ti=k - 3: lnB_c(ti))
            if k - 2 >= 0 and k - 2 < 16:
                pre.append(lambda ti=k - 2: lnB_p(ti))
            if k - 1 >= 0 and k - 1 < 16:
                pre.append(lambda ti=k - 1: (lnA_c(ti), emit_wd_loads(2)))
            if k < 16:
                pre.append(lambda ti=k: lnA_p(ti))
        pre.append(lambda: emit_wd_loads(20))
        for idx in range(8):
            pre.append(lambda idx=idx: qk_pass(idx))
        pre.append(sum_prep)
        pre.append(sum_chunks)
        for c0 in range(0, NCH, 2):
            pre.append(lambda c0=c0: [d2a_chunk(c) for c in range(c0, c0 + 2)])
        pre.append(lambda: (d2_gates(), d2_ba()))
        for k in range(NCH + 2):
            pre.append(lambda k=k: (
                d2b_a(k) if k < NCH else None,
                d2b_b(k - 1) if 0 <= k - 1 < NCH else None,
                d2b_c(k - 2) if 0 <= k - 2 < NCH else None))

        t_gu_sb = {}
        t_gu_hc = {}
        s_silu = {}
        v_h = {}
        s_fs = {}
        v_fin = {}
        st_done = {}
        xt_issue = {}

        def Gps(p):
            return pb[p][:, 0:512]

        def Ups(p):
            return pb[p][:, 512:1024]

        def load_gu(idx2):
            hc2 = idx2 % NHC
            gp2 = hc2 % 3
            for jh in range(2):
                g.op(lambda e, hc=hc2, gpar=gp2, jh=jh: e.dma_start(
                    out=gu_bf[gpar][:, jh * 1024:(jh + 1) * 1024],
                    in_=wgu[hc][:, jh * 1024:(jh + 1) * 1024]),
                    (dGUg if jh == 0 else dGU)[gp2], 16)

        def f_e1(half, hc, nh, pos):
            idx = half * NHC + hc
            gpar = hc % 3
            gsb = pos
            p = gsb % 2 if half == 0 else gsb % 4
            DEP = 2 if half == 0 else 4
            if nh == 0:
                t.wait(dGUg[gpar], 16 * (hc // 3 + 1 + (half * NHC) // 3))
            t.wait(V, apply_done[half * 4 + nh * 2 + 1])
            if half == 1 and hc == 0:
                t.wait(S, s_fs[7])
                t.wait(V, v_fin[7])
                if nh == 1:
                    t.wait(V, v_sdone[NCH - 1])
            elif half == 1 and hc == 1:
                t.wait(V, v_sdone[NCH - 1])
            elif gsb >= DEP:
                t.wait(S, s_silu[gsb - DEP])
                t.wait(V, v_h[gsb - DEP])
            t_gate = None
            for gu in range(2):
                if gu == 1 and nh == 0:
                    t.wait(dGU[gpar], 16 * (hc // 3 + 1 + (half * NHC) // 3))
                o_ps = Gps(p) if gu == 0 else Ups(p)
                for dc in range(NDC):
                    sp = dc == NDC - 1
                    fn = (lambda e, o=o_ps, gu=gu, dc=dc, gpar=gpar,
                          half=half, nh=nh, st=(dc == 0), spm=sp:
                          e.matmul(out=o,
                                   lhsT=gu_bf[gpar][:, gu * 1024 + dc * 128: gu * 1024 + (dc + 1) * 128],
                                   rhs=xn[:, dc * N + half * 1024 + nh * 512: dc * N + half * 1024 + (nh + 1) * 512],
                                   start=st, stop=spm))
                    if sp:
                        if gu == 0:
                            t_gate = t.op(fn, T, 1)
                        else:
                            t_gu_sb[gsb] = t.op(fn, T, 1)
                    else:
                        t.op(fn)
            if nh == 1:
                t_gu_hc[idx] = t_gu_sb[gsb]
                if idx + 3 < 2 * NHC:
                    g.wait(T, t_gu_hc[idx])
                    load_gu(idx + 3)
            s_pre("Silu")
            s.wait(T, t_gate)
            if gsb >= 2:
                s.wait(V, v_h[gsb - 2])
            sgp = gsb % 2
            s_silu[gsb] = s.op(lambda e, sgp=sgp, a=Gps(p):
                               e.activation(out=sg_bf[:, sgp * 512:(sgp + 1) * 512],
                                            in_=a, func=AF.Silu, scale=QS), S, 1)
            v.wait(S, s_silu[gsb])
            v.wait(T, t_gu_sb[gsb])
            v_h[gsb] = v.op(lambda e, sgp=sgp, hc=hc, nh=nh, a=Ups(p):
                            e.scalar_tensor_tensor(
                                out=h_bf[:, hc * 1024 + nh * 512: hc * 1024 + nh * 512 + 512],
                                in0=a, scalar=QS,
                                in1=sg_bf[:, sgp * 512:(sgp + 1) * 512],
                                op0=OP.mult, op1=OP.mult), V, 1)

        def f_e2(half, nc2):
            idx2 = half * 8 + nc2
            r0 = half * 1024 + nc2 * 128
            dn_ps = (pb[0] if nc2 % 2 == 0 else pb[3]) if half == 0 else pb[nc2 % 2]
            s_ps = pb[1] if half == 0 else pb[2]
            if idx2 >= 2:
                y.wait(dOT[nc2 % 2], st_done[idx2 - 2])
            else:
                y.wait(V, v_wsmall)
            y.op(lambda e, dpar=nc2 % 2, r0=r0:
                 e.dma_start(out=xtmt[dpar][:], in_=xtm[r0:r0 + 128, :]),
                 dXT[nc2 % 2], 16)
            xt_issue[idx2] = dXT[nc2 % 2].n
            gsb_last = half * 2 * NHC + 2 * NHC - 1
            if nc2 == 0:
                t.wait(V, v_h[gsb_last])
                t.wait(T, 0)
                t.wait(dWD, 16 * NHC)
                t.wait(V, v_sdone[7 if half == 0 else 15])
                t.wait(S, s_silu[gsb_last - 1])
                t.wait(V, v_h[gsb_last - 1])
            elif half == 0:
                if nc2 == 1:
                    t.wait(V, v_pt[NCH - 1])
                else:
                    t.wait(S, s_fs[idx2 - 2])
            else:
                if nc2 == 1:
                    gl = half * 2 * NHC + 2 * NHC - 1
                    t.wait(S, s_silu[gl])
                    t.wait(V, v_h[gl])
                if nc2 >= 2:
                    t.wait(S, s_fs[idx2 - 2])
            def dn_group(dh):
                for hc in range(NHC):
                    sp = hc == NHC - 1
                    fn = (lambda e, o=dn_ps, dh=dh, hc=hc, nc2=nc2,
                          st=(hc == 0), spm=sp:
                          e.matmul(out=o[:, dh * 512:(dh + 1) * 512],
                                   lhsT=h_bf[:, hc * 1024 + nc2 * 128: hc * 1024 + (nc2 + 1) * 128],
                                   rhs=wd_bf[:, hc * 1024 + dh * 512: hc * 1024 + dh * 512 + 512],
                                   start=st, stop=spm))
                    if sp:
                        t_dn[dh] = t.op(fn, T, 1)
                    else:
                        t.op(fn)

            t_dn = [None, None]
            dn_group(0)
            if idx2 >= 1:
                t.wait(V, v_fin[idx2 - 1])
            t_wo = None
            for dh in range(2):
                fn = (lambda e, o=s_ps, dh=dh, r0=r0:
                      e.matmul(out=o[:, dh * 512:(dh + 1) * 512],
                               lhsT=outsa[:, r0:r0 + 128],
                               rhs=wob_bf[:, dh * 512:(dh + 1) * 512],
                               start=True, stop=True))
                if dh == 1:
                    t_wo = t.op(fn, T, 1)
                else:
                    t.op(fn)
            dn_group(1)
            vm = [None, None]
            for dh in range(2):
                if dh == 0:
                    s_pre("Silu")
                s.wait(T, t_dn[dh])
                if idx2 >= 1 and dh == 0:
                    s.wait(V, v_fin[idx2 - 1])
                s_fs[idx2] = s.op(
                    lambda e, dh=dh, a=dn_ps:
                    e.activation(out=gf_sb[:, dh * 512:(dh + 1) * 512],
                                 in_=a[:, dh * 512:(dh + 1) * 512],
                                 func=AF.Silu, scale=QS), S, 1)
                v.wait(S, s_fs[idx2])
                if dh == 0:
                    v.wait(T, t_wo)
                    v.wait(dXT[nc2 % 2], xt_issue[idx2])
                vm[dh] = v.op(
                    lambda e, dh=dh, a=s_ps:
                    e.tensor_tensor(out=gf_sb[:, dh * 512:(dh + 1) * 512],
                                    in0=gf_sb[:, dh * 512:(dh + 1) * 512],
                                    in1=a[:, dh * 512:(dh + 1) * 512],
                                    op=OP.mult), V, 1)
                v.wait(V, vm[dh])
                v_fin[idx2] = v.op(
                    lambda e, dh=dh, dpar=nc2 % 2:
                    e.tensor_tensor(out=xtmt[dpar][:, dh * 512:(dh + 1) * 512],
                                    in0=gf_sb[:, dh * 512:(dh + 1) * 512],
                                    in1=xtmt[dpar][:, dh * 512:(dh + 1) * 512],
                                    op=OP.add), V, 1)
            y.wait(V, v_fin[idx2])
            y.op(lambda e, dpar=nc2 % 2, r0=r0:
                 e.dma_start(out=out[r0:r0 + 128, :], in_=xtmt[dpar][:]),
                 dOT[nc2 % 2], 16)
            st_done[idx2] = dOT[nc2 % 2].n

        ffn = []
        for half in range(2):
            order = [(hc, nh) for hc in range(NHC) for nh in range(2)]
            for k, (hc, nh) in enumerate(order):
                pos = half * 2 * NHC + k
                ffn.append(lambda half=half, hc=hc, nh=nh, pos=pos:
                           f_e1(half, hc, nh, pos))
            for nc2 in range(8):
                ffn.append(lambda half=half, nc2=nc2: f_e2(half, nc2))

        pre[0]()
        load_gu(0)
        pre[1]()
        pre[2]()
        load_gu(1)
        pre[3]()
        pre[4]()
        pre[5]()
        load_gu(2)
        for fn in pre[6:10]:
            fn()
        ffn[0]()
        decay_setup()
        for fn in pre[10:16]:
            fn()
        ffn[1]()
        pi = 16
        for fi, fslice in enumerate(ffn[2:]):
            if fi == 46:
                assert pi >= len(pre), (pi, len(pre))
            fslice()
            take = 2 if pi < len(pre) else 0
            for _ in range(take):
                if pi < len(pre):
                    pre[pi]()
                    pi += 1
        while pi < len(pre):
            pre[pi]()
            pi += 1

        y.wait(dOT[0], dOT[0].n)
        y.wait(dOT[1], dOT[1].n)

        block = stack.enter_context(nc.Block())

        @block.tensor
        def _(e):
            for f in ops["t"]:
                f(e)

        @block.vector
        def _(e):
            for f in ops["v"]:
                f(e)

        @block.scalar
        def _(e):
            for f in ops["s"]:
                f(e)

        @block.gpsimd
        def _(e):
            for f in ops["g"]:
                f(e)

        @block.sync
        def _(e):
            for f in ops["y"]:
                f(e)

    return nc


_CACHE = {}


def _get_nc():
    if "nc" not in _CACHE:
        nc = build_nc()
        nc.finalize()
        _CACHE["nc"] = nc
    return _CACHE["nc"]


def make_in_maps(L0_attn_x, L0_attn_Wq, L0_attn_Wk, L0_attn_Wv, L0_attn_Wo,
                 L0_attn_bo, L0_attn_Wg, L0_attn_bg, L0_attn_decay_param,
                 L0_ffn_w_gate, L0_ffn_w_up, L0_ffn_w_down):
    f32 = np.float32
    x = np.asarray(L0_attn_x, f32)
    Wq = np.asarray(L0_attn_Wq, f32)
    Wk = np.asarray(L0_attn_Wk, f32)
    Wv = np.asarray(L0_attn_Wv, f32)
    Wo = np.asarray(L0_attn_Wo, f32)
    bo = np.asarray(L0_attn_bo, f32)
    Wg = np.asarray(L0_attn_Wg, f32)
    bg = np.asarray(L0_attn_bg, f32)
    dp = np.asarray(L0_attn_decay_param, f32)
    def low_bytes(a):
        a = np.ascontiguousarray(np.asarray(a, np.int32))
        return np.ascontiguousarray(a.view(np.int8).reshape(*a.shape, 4)[..., 0])

    wg_i = low_bytes(L0_ffn_w_gate)
    wu_i = low_bytes(L0_ffn_w_up)
    wd_i = low_bytes(L0_ffn_w_down)

    wqk_c = np.concatenate([Wq, Wk], axis=1)
    wqk_t = np.ascontiguousarray(
        wqk_c.reshape(NDC, 128, 128).transpose(1, 0, 2).reshape(128, NDC * 128))
    wkvg_c = np.zeros((D, 160), f32)
    wkvg_c[:, 0:64] = Wk
    wkvg_c[:, 64:128] = Wv
    wkvg_c[:, 128:129] = Wg
    wkvg_t = np.ascontiguousarray(
        wkvg_c.reshape(NDC, 128, 160).transpose(1, 0, 2).reshape(128, NDC * 160))
    wob_c = np.ascontiguousarray(np.concatenate([Wo, bo[None, :]], axis=0))
    wgT = np.ascontiguousarray(wg_i.T)
    wuT = np.ascontiguousarray(wu_i.T)
    wg_tile = wgT.reshape(NDC, 128, NHC, 128).transpose(2, 1, 0, 3).reshape(NHC, 128, 1024)
    wu_tile = wuT.reshape(NDC, 128, NHC, 128).transpose(2, 1, 0, 3).reshape(NHC, 128, 1024)
    wgu_c = np.ascontiguousarray(
        np.concatenate([wg_tile, wu_tile], axis=2))
    wdt_c = np.ascontiguousarray(wd_i.T.reshape(NHC, 128, 1024))
    dec_c = np.ascontiguousarray(dp.reshape(1, 1))
    bgs_c = np.ascontiguousarray(bg.reshape(1, 1))

    in_maps = []
    for core in range(8):
        b, hf = core // 2, core % 2
        def tile_major(xt):
            return np.ascontiguousarray(
                xt.reshape(NDC, 128, 8, TL).transpose(2, 1, 0, 3)
                .reshape(8, 128, NDC * TL))
        xs = np.ascontiguousarray(x[b, hf * N:(hf + 1) * N, :])
        xf = tile_major(xs.T)
        if hf == 1:
            xp = tile_major(np.ascontiguousarray(x[b, 0:N, :].T))
        else:
            xp = np.zeros((8, 128, NDC * TL), f32)
        in_maps.append({
            "xfm": xf, "xpfm": xp, "xtm": xs,
            "wqk": wqk_t, "wkvg": wkvg_t, "wob": wob_c,
            "wgu": wgu_c, "wdt": wdt_c, "dec": dec_c, "bgs": bgs_c,
        })
    return in_maps


def kernel(L0_attn_x, L0_attn_Wq, L0_attn_Wk, L0_attn_Wv, L0_attn_Wo,
           L0_attn_bo, L0_attn_Wg, L0_attn_bg, L0_attn_decay_param,
           L0_ffn_w_gate, L0_ffn_w_up, L0_ffn_w_down):
    in_maps = make_in_maps(L0_attn_x, L0_attn_Wq, L0_attn_Wk, L0_attn_Wv,
                           L0_attn_Wo, L0_attn_bo, L0_attn_Wg, L0_attn_bg,
                           L0_attn_decay_param, L0_ffn_w_gate, L0_ffn_w_up,
                           L0_ffn_w_down)
    nc = _get_nc()
    res = run_bass_kernel_spmd(nc, in_maps, core_ids=list(range(8)))
    out = np.empty((4, 4096, 1024), np.float32)
    for core in range(8):
        b, hf = core // 2, core % 2
        out[b, hf * N:(hf + 1) * N, :] = res.results[core]["out"]
    return out


# version 49
# speedup vs baseline: 1.0059x; 1.0059x over previous
import numpy as np
import ml_dtypes
from contextlib import ExitStack

import concourse.bass as bass
import concourse.mybir as mybir
from concourse.bass_utils import run_bass_kernel_spmd

F32 = mybir.dt.float32
BF16 = mybir.dt.bfloat16
I32 = mybir.dt.int32
I8 = mybir.dt.int8
AF = mybir.ActivationFunctionType
OP = mybir.AluOpType

D = 1024
M = 64
H = 3072
N = 2048
CH = 128
NCH = N // CH
NDC = D // 128
NHC = H // 128
TL = 256
QS = float(np.float32(1.0 / 127.0))
EPS = 1e-5
DECAY_MIN, DECAY_MAX = 0.9, 0.999


class Sem:
    def __init__(self, handle):
        self.sem = handle
        self.n = 0


class Eng:

    def __init__(self, ops):
        self.ops = ops

    def wait(self, ctr, val):
        if val <= 0:
            return
        self.ops.append(lambda e, s=ctr.sem, v=val: e.wait_ge(s, v))

    def op(self, fn, inc=None, k=1):
        if inc is None:
            self.ops.append(fn)
            return None
        inc.n += k
        self.ops.append(lambda e, f=fn, s=inc.sem, kk=k: f(e).then_inc(s, kk))
        return inc.n

    def drain(self):
        self.ops.append(lambda e: e.drain())


class View:

    def __init__(self, base, c0):
        self.base = base
        self.c0 = c0

    def __getitem__(self, idx):
        rows, cols = idx
        if isinstance(cols, slice):
            cols = slice(self.c0 + (cols.start or 0), self.c0 + cols.stop)
        return self.base[rows, cols]


def build_nc():
    nc = bass.Bass("TRN2", target_bir_lowering=False)

    xfm = nc.dram_tensor("xfm", [8, 128, NDC * TL], F32, kind="ExternalInput").ap()
    xpfm = nc.dram_tensor("xpfm", [8, 128, NDC * TL], F32, kind="ExternalInput").ap()
    xtm = nc.dram_tensor("xtm", [N, D], F32, kind="ExternalInput").ap()
    wqk = nc.dram_tensor("wqk", [128, NDC * 128], F32, kind="ExternalInput").ap()
    wkvg = nc.dram_tensor("wkvg", [128, NDC * 160], F32, kind="ExternalInput").ap()
    wob = nc.dram_tensor("wob", [M + 1, D], F32, kind="ExternalInput").ap()
    wgu = nc.dram_tensor("wgu", [NHC, 128, 2048], I8, kind="ExternalInput").ap()
    wdt = nc.dram_tensor("wdt", [NHC, 128, 1024], I8, kind="ExternalInput").ap()
    dec = nc.dram_tensor("dec", [1, 1], F32, kind="ExternalInput").ap()
    bgs = nc.dram_tensor("bgs", [1, 1], F32, kind="ExternalInput").ap()
    out = nc.dram_tensor("out", [N, D], F32, kind="ExternalOutput").ap()

    with ExitStack() as stack:
        E_ = stack.enter_context

        xn = E_(nc.sbuf_tensor("xn", [128, NDC * N], BF16))
        wd_bf = E_(nc.sbuf_tensor("wd_bf", [128, NHC * 1024], BF16))
        h_bf = E_(nc.sbuf_tensor("h_bf", [128, NHC * 1024], BF16))
        xstage = [E_(nc.sbuf_tensor(f"xstage{i}", [128, NDC * TL], BF16)) for i in range(3)]
        xsq_f = [E_(nc.sbuf_tensor(f"xsqf{i}", [128, NDC * TL], BF16)) for i in range(2)]
        m_sb = [E_(nc.sbuf_tensor(f"msb{i}", [128, TL], F32)) for i in range(2)]
        gu_bf = [E_(nc.sbuf_tensor(f"gubf{i}", [128, 2048], BF16)) for i in range(3)]
        qt_bf = E_(nc.sbuf_tensor("qt_bf", [M, N], BF16))
        k_bf = E_(nc.sbuf_tensor("k_bf", [M, N], BF16))
        kv_bf = E_(nc.sbuf_tensor("kv_bf", [128, NCH * 128], BF16))
        outsa = E_(nc.sbuf_tensor("outsa", [M + 1, N], BF16))
        zcol = [E_(nc.sbuf_tensor(f"zcol{i}", [128, NCH], F32)) for i in range(2)]
        gcol = [E_(nc.sbuf_tensor(f"gcol{i}", [128, NCH], F32)) for i in range(2)]
        gwcol = [E_(nc.sbuf_tensor(f"gwcol{i}", [128, NCH], F32)) for i in range(2)]
        gdcol = E_(nc.sbuf_tensor("gdcol", [128, NCH], F32))
        vtmp = [E_(nc.sbuf_tensor(f"vtmp{i}", [128, TL], F32)) for i in range(2)]
        rstd = [E_(nc.sbuf_tensor(f"rstd{i}", [128, TL], F32)) for i in range(2)]
        abig = E_(nc.sbuf_tensor("abig", [128, NDC * TL // 2], F32))
        wqk_bf = E_(nc.sbuf_tensor("wqk_bf", [128, NDC * 128], BF16))
        wkvg_bf = E_(nc.sbuf_tensor("wkvg_bf", [128, NDC * 160], BF16))
        wob_bf = E_(nc.sbuf_tensor("wob_bf", [M + 1, D], BF16))
        xtmt = [E_(nc.sbuf_tensor(f"xtmt{i}", [128, D], F32)) for i in range(2)]
        gf_sb = abig
        sg_bf = E_(nc.sbuf_tensor("sgbf", [128, D], BF16))
        pt_bf = [E_(nc.sbuf_tensor(f"ptbf{i}", [128, 128], BF16)) for i in range(2)]
        tril_bf = E_(nc.sbuf_tensor("tril_bf", [128, 128], BF16))
        c1024 = E_(nc.sbuf_tensor("c1024", [128, 128], BF16))
        eps_c = E_(nc.sbuf_tensor("eps_c", [128, 1], F32))
        dec_s = E_(nc.sbuf_tensor("dec_s", [1, 1], F32))
        bgs_s = E_(nc.sbuf_tensor("bgs_s", [1, 1], F32))
        sig_s = E_(nc.sbuf_tensor("sig_s", [1, 1], F32))
        dcy_s = E_(nc.sbuf_tensor("dcy_s", [1, 1], F32))
        tscr = E_(nc.sbuf_tensor("tscr", [1, 1], F32))
        ld_s = E_(nc.sbuf_tensor("ld_s", [1, 1], F32))
        ones_r = E_(nc.sbuf_tensor("ones_r", [1, 128], F32))
        ldbc = E_(nc.sbuf_tensor("ldbc", [128, 1], F32))
        bgbc = E_(nc.sbuf_tensor("bgbc", [128, 1], F32))
        w64 = E_(nc.sbuf_tensor("w64", [M, 128], F32))
        wr_col = E_(nc.sbuf_tensor("wr_col", [128, 1], F32))
        dneg_col = E_(nc.sbuf_tensor("dneg_col", [128, 1], F32))
        dC_col = E_(nc.sbuf_tensor("dC_col", [128, 1], F32))
        S_bf = [E_(nc.sbuf_tensor(f"Sbf{i}", [M, M], BF16)) for i in range(2)]
        wrp_ext = E_(nc.sbuf_tensor("wrp_ext", [128, NCH], F32))

        xnp_w = [View(wd_bf, 20 * 1024), View(wd_bf, 22 * 1024)]

        pb = [E_(nc.psum_tensor(f"pb{i}", [128, 1024], F32)) for i in range(4)]

        mk = lambda name: Sem(E_(nc.semaphore(name)))
        dSM = mk("dSM")
        dSM2 = mk("dSM2")
        dX = [mk("dXA"), mk("dXB"), mk("dXC")]
        dX2 = [mk("dXA2"), mk("dXB2"), mk("dXC2")]
        dGU = [mk("dGU0"), mk("dGU1"), mk("dGU2")]
        dGUg = [mk("dGUg0"), mk("dGUg1"), mk("dGUg2")]
        dWD = mk("dWD")
        dXT = [mk("dXT0"), mk("dXT1")]
        dOT = [mk("dOT0"), mk("dOT1")]
        T = mk("T"); V = mk("V"); S = mk("S"); G = mk("G")

        ops = {k: [] for k in "tvsgy"}
        t = Eng(ops["t"]); v = Eng(ops["v"]); s = Eng(ops["s"])
        g = Eng(ops["g"]); y = Eng(ops["y"])

        st8 = {}

        cur_tset = ["setup"]
        dummy_last = [0]

        def s_pre(fname):
            if cur_tset[0] != fname:
                cur_tset[0] = fname
                s.wait(S, dummy_last[0])
                dummy_last[0] = s.op(lambda e, f=fname: e.activation(
                    out=tscr[:], in_=tscr[:], func=getattr(AF, f)), S, 1)

        for dst, src in [(dec_s[:], dec), (bgs_s[:], bgs)]:
            s.op(lambda e, d_=dst, s_=src: e.dma_start(out=d_, in_=s_), dSM, 16)
        scalars_done = dSM.n
        s.op(lambda e: e.dma_start(out=abig[:], in_=wqk), dSM, 16)
        smalls_done = dSM.n
        for dst, src in [(xtmt[0][:], wkvg[:, 0:1024]),
                         (xtmt[1][:, 0:256], wkvg[:, 1024:1280])]:
            s.op(lambda e, d_=dst, s_=src: e.dma_start(out=d_, in_=s_), dSM2, 16)
        kvg_done = dSM2.n

        g.op(lambda e: e.iota(out=wr_col[:], pattern=[[1, 1]], base=127,
                              channel_multiplier=-1,
                              allow_small_or_imprecise_dtypes=True))
        g.op(lambda e: e.iota(out=dneg_col[:], pattern=[[1, 1]], base=-1,
                              channel_multiplier=-1,
                              allow_small_or_imprecise_dtypes=True))
        g.op(lambda e: e.iota(out=w64[:], pattern=[[1, 128]], base=1,
                              channel_multiplier=0,
                              allow_small_or_imprecise_dtypes=True))
        g.op(lambda e: e.iota(out=wrp_ext[:], pattern=[[-128, NCH]], base=2047,
                              channel_multiplier=-1,
                              allow_small_or_imprecise_dtypes=True))
        g.op(lambda e: e.memset(tril_bf[:], 1.0))
        g.drain()
        g_setup = g.op(lambda e: e.affine_select(
            out=tril_bf[:], in_=tril_bf[:], pattern=[[1, 128]],
            compare_op=OP.is_ge, fill=0.0, base=0, channel_multiplier=-1), G, 1)

        v.op(lambda e: e.memset(eps_c[:], EPS))
        v.op(lambda e: e.memset(c1024[:], 1.0 / 1024.0))
        v.op(lambda e: e.memset(vtmp[0][0:1, 0:128], 1.0))
        v.op(lambda e: e.memset(ones_r[:], 1.0))
        v.op(lambda e: e.memset(outsa[M:M + 1, :], 1.0))
        v.op(lambda e: e.memset(S_bf[0][:], 0.0), V, 1)
        v_const = V.n
        v.wait(dSM, smalls_done)
        v_wqk = v.op(lambda e: e.tensor_copy(out=wqk_bf[:], in_=abig[:]), V, 1)
        v.wait(dSM2, kvg_done)
        v.op(lambda e: e.tensor_copy(out=wkvg_bf[:, 0:1024], in_=xtmt[0][:]))
        v_w1 = v.op(lambda e: e.tensor_copy(out=wkvg_bf[:, 1024:1280],
                                            in_=xtmt[1][:, 0:256]), V, 1)
        s.wait(V, v_wqk)
        s.op(lambda e: e.dma_start(out=abig[0:M + 1, :], in_=wob), dSM, 16)
        smalls2 = dSM.n
        v.wait(dSM, smalls2)
        v.op(lambda e: e.tensor_copy(out=wob_bf[:], in_=abig[0:M + 1, :]), V, 1)
        v_wsmall = V.n

        s.wait(V, v_const)
        dummy_last[0] = s.op(lambda e: e.activation(
            out=tscr[:], in_=eps_c[0:1, 0:1], func=AF.Square), S, 1)

        def decay_setup():
            s.wait(dSM, scalars_done)
            s_sig = s.op(lambda e: e.activation(out=sig_s[:], in_=dec_s[:],
                                                func=AF.Sigmoid), S, 1)
            v.wait(S, s_sig)
            v_dcy = v.op(lambda e: e.tensor_scalar(out=dcy_s[:], in0=sig_s[:],
                                                   scalar1=DECAY_MAX - DECAY_MIN,
                                                   scalar2=DECAY_MIN, op0=OP.mult,
                                                   op1=OP.add), V, 1)
            s.wait(V, v_dcy)
            s_ld = s.op(lambda e: e.activation(out=ld_s[:], in_=dcy_s[:],
                                               func=AF.Ln), S, 1)
            t.wait(S, s_ld)
            t.wait(V, v_const)
            t.op(lambda e: e.matmul(out=pb[3][:, 448:449], lhsT=ones_r[:],
                                    rhs=ld_s[:], start=True, stop=True))
            t_bcast = t.op(lambda e: e.matmul(out=pb[3][:, 960:961],
                                              lhsT=ones_r[:],
                                              rhs=bgs_s[:], start=True, stop=True),
                           T, 1)
            v.wait(T, t_bcast)
            v.wait(G, g_setup)
            v.op(lambda e: e.tensor_copy(out=ldbc[:], in_=pb[3][:, 448:449]))
            v.op(lambda e: e.tensor_copy(out=bgbc[:], in_=pb[3][:, 960:961]))
            v.drain()
            v.op(lambda e: e.tensor_scalar(out=w64[:], in0=w64[:],
                                           scalar1=ldbc[0:M, :], scalar2=None,
                                           op0=OP.mult))
            v.op(lambda e: e.tensor_scalar(out=wr_col[:], in0=wr_col[:],
                                           scalar1=ldbc[:], scalar2=None,
                                           op0=OP.mult))
            v.op(lambda e: e.tensor_scalar(out=wrp_ext[:], in0=wrp_ext[:],
                                           scalar1=ldbc[:], scalar2=None,
                                           op0=OP.mult))
            v.op(lambda e: e.tensor_scalar(out=dneg_col[:], in0=dneg_col[:],
                                           scalar1=ldbc[:], scalar2=None,
                                           op0=OP.mult))
            v_preexp = v.op(lambda e: e.tensor_scalar(out=dC_col[:], in0=ldbc[:],
                                                      scalar1=128.0, scalar2=None,
                                                      op0=OP.mult), V, 1)
            s.wait(V, v_preexp)
            s.op(lambda e: e.activation(out=w64[:], in_=w64[:], func=AF.Exp))
            s.op(lambda e: e.activation(out=wr_col[:], in_=wr_col[:], func=AF.Exp))
            s.op(lambda e: e.activation(out=wrp_ext[:], in_=wrp_ext[:], func=AF.Exp))
            s.op(lambda e: e.activation(out=dneg_col[:], in_=dneg_col[:],
                                        func=AF.Exp))
            s_decay = s.op(lambda e: e.activation(out=dC_col[:], in_=dC_col[:],
                                                  func=AF.Exp), S, 1)
            st8["s_decay"] = s_decay
            st8["v_preexp"] = v_preexp

        apply_done = [0] * 16
        ln_mm_done = [0] * 16
        x_issue = [0] * 16
        x_issue2 = [0] * 16
        kvgp_evac = [0] * NCH
        kvgp_mm = [0] * NCH
        kvgo_evac = [0] * NCH
        qk_evac = [0] * 8
        t_qk = [0] * 8
        v_ba_p = [0] * NCH
        v_ba = [0] * NCH
        v_pt = [0] * NCH
        v_sdone = [0] * NCH
        t_p = [0] * NCH
        wd_pending = list(range(20))
        sparholder = [0]

        def emit_wd_loads(k):
            for _ in range(k):
                if not wd_pending:
                    return
                hc = wd_pending.pop(0)
                g.op(lambda e, hc=hc:
                     e.dma_start(out=wd_bf[:, hc * 1024:(hc + 1) * 1024],
                                 in_=wdt[hc]), dWD, 16)

        def m_ps_of(par):
            return pb[2][:, par * 512: par * 512 + TL]

        def s2_ps_of(par):
            return pb[3][:, par * 512: par * 512 + TL]

        sq_done = [0] * 16

        def lnA_p(ti):
            p3 = ti % 3
            own = ti < 8
            tt = ti if own else ti - 8
            if ti >= 3:
                g.wait(V, apply_done[ti - 3])
            src = (xfm if own else xpfm)[tt]
            g.op(lambda e, d_=xstage[p3][:, 0:1024], s_=src[:, 0:1024]:
                 e.dma_start(out=d_, in_=s_), dX[p3], 16)
            x_issue[ti] = dX[p3].n
            g.op(lambda e, d_=xstage[p3][:, 1024:2048], s_=src[:, 1024:2048]:
                 e.dma_start(out=d_, in_=s_), dX2[p3], 16)
            x_issue2[ti] = dX2[p3].n
            s.wait(dX[p3], x_issue[ti])
            if ti >= 2:
                s.wait(T, ln_mm_done[ti - 2])
            sq_h = [0, 0]
            for jh in range(2):
                if jh == 1:
                    s.wait(dX2[p3], x_issue2[ti])
                sq_h[jh] = s.op(
                    lambda e, jh=jh, a=xsq_f[ti % 2], b=xstage[p3]:
                    e.activation(out=a[:, jh * 1024:(jh + 1) * 1024],
                                 in_=b[:, jh * 1024:(jh + 1) * 1024],
                                 func=AF.Square), S, 1)
            sq_done[ti] = tuple(sq_h)

        def lnA_c(ti):
            par = ti % 2
            p3 = ti % 3
            m_ps, s2_ps = m_ps_of(par), s2_ps_of(par)
            t.wait(dX[p3], x_issue[ti])
            if ti >= 2:
                t.wait(V, st8[("v_var", ti - 2)])
            if ti >= 10:
                t.wait(V, kvgp_evac[2 * (ti - 2 - 8) + 1])
            for dc in range(NDC):
                if dc == NDC // 2:
                    t.wait(dX2[p3], x_issue2[ti])
                t.op(lambda e, o=m_ps, a=xstage[p3][:, dc * TL:(dc + 1) * TL],
                     st=(dc == 0), sp=(dc == NDC - 1):
                     e.matmul(out=o, lhsT=c1024[:], rhs=a, start=st, stop=sp))
            t.wait(S, sq_done[ti][0])
            for dc in range(NDC):
                if dc == NDC // 2:
                    t.wait(S, sq_done[ti][1])
                last = (lambda e, o=s2_ps,
                        a=xsq_f[ti % 2][:, dc * TL:(dc + 1) * TL],
                        st=(dc == 0), sp=(dc == NDC - 1):
                        e.matmul(out=o, lhsT=c1024[:], rhs=a, start=st, stop=sp))
                if dc == NDC - 1:
                    ln_mm_done[ti] = t.op(last, T, 1)
                else:
                    t.op(last)

        aph = {}

        def lnB_p(ti):
            par = ti % 2
            p3 = ti % 3
            own = ti < 8
            tt = ti if own else ti - 8
            m_ps, s2_ps = m_ps_of(par), s2_ps_of(par)
            s.wait(T, ln_mm_done[ti])
            if ti - 2 >= 8:
                s.wait(T, kvgp_mm[2 * (ti - 2 - 8) + 1])
            s_msq = s.op(lambda e, o=vtmp[par], a=m_ps:
                         e.activation(out=o[:], in_=a, func=AF.Square), S, 1)
            s_pre("Sqrt")
            v.wait(S, s_msq)
            v.op(lambda e, o=m_sb[par], a=m_ps:
                 e.tensor_copy(out=o[:], in_=a))
            v_var = v.op(lambda e, o=vtmp[par], a=s2_ps:
                         e.tensor_tensor(out=o[:], in0=a, in1=o[:],
                                         op=OP.subtract), V, 1)
            st8[("v_var", ti)] = v_var
            s.wait(V, v_var)
            s_std = s.op(lambda e, o=vtmp[par]:
                         e.activation(out=o[:], in_=o[:], func=AF.Sqrt,
                                      bias=eps_c[:]), S, 1)
            v.wait(S, s_std)
            v.op(lambda e, o=rstd[par], a=vtmp[par]:
                 e.reciprocal(out=o[:], in_=a[:]), V, 1)
            if not own and ti >= 10:
                v.wait(T, kvgp_mm[2 * (ti - 2 - 8) + 1])
            if ti >= 1:
                v.wait(V, apply_done[ti - 1])
            else:
                v.wait(V, v_wsmall)
            last = 0
            ap_half = [0, 0]
            for j in range(2):
                xin3 = bass.AP(xstage[p3], j * 128,
                               [[NDC * TL, 128], [TL, NDC], [1, 128]])
                ab3 = bass.AP(abig, 0, [[NDC * 128, 128], [128, NDC], [1, 128]])
                m_bc = bass.AP(m_sb[par], j * 128,
                               [[TL, 128], [0, NDC], [1, 128]])
                r_bc = bass.AP(rstd[par], j * 128,
                               [[TL, 128], [0, NDC], [1, 128]])
                if own:
                    dst3 = bass.AP(xn, tt * TL + j * 128,
                                   [[NDC * N, 128], [N, NDC], [1, 128]])
                else:
                    c0 = (20 + 2 * par) * 1024 + j * 128
                    dst3 = bass.AP(wd_bf, c0,
                                   [[NHC * 1024, 128], [TL, NDC], [1, 128]])
                if j == 1:
                    v.wait(V, last)
                v_sub = v.op(lambda e, o=ab3, a=xin3, b=m_bc:
                             e.tensor_tensor(out=o, in0=a, in1=b,
                                             op=OP.subtract), V, 1)
                v.wait(V, v_sub)
                last = v.op(lambda e, o=dst3, a=ab3, b=r_bc:
                            e.tensor_tensor(out=o, in0=a, in1=b, op=OP.mult),
                            V, 1)
                ap_half[j] = last
            aph[ti] = ap_half
            apply_done[ti] = last

        def lnB_c(ti):
            par = ti % 2
            own = ti < 8
            tt = ti if own else ti - 8
            ap_half = aph[ti]
            if not own:
                for j in range(2):
                    cc = 2 * tt + j
                    kvg_ps = (pb[2] if j == 0 else pb[3])[
                        :, par * 512 + 256: par * 512 + 256 + 129]
                    t.wait(V, ap_half[j])
                    t.wait(V, v_wsmall)
                    if cc >= 2:
                        t.wait(V, kvgp_evac[cc - 2])
                    t_mm = None
                    for dc in range(NDC):
                        sp = dc == NDC - 1
                        fn = (lambda e, o=kvg_ps, dc=dc, par=par, j=j,
                              st=(dc == 0), sp=sp:
                              e.matmul(out=o,
                                       lhsT=xnp_w[par][:, dc * TL + j * 128: dc * TL + j * 128 + 128],
                                       rhs=wkvg_bf[:, dc * 160: dc * 160 + 129],
                                       start=st, stop=sp))
                        if sp:
                            t_mm = t.op(fn, T, 1)
                        else:
                            t.op(fn)
                    kvgp_mm[cc] = t_mm
                    v.wait(T, t_mm)
                    v.op(lambda e, o=kv_bf[:, cc * 128:(cc + 1) * 128],
                         a=kvg_ps: e.tensor_copy(out=o, in_=a[:, 0:128]))
                    kvgp_evac[cc] = v.op(
                        lambda e, o=zcol[0][:, cc:cc + 1], a=kvg_ps:
                        e.tensor_copy(out=o, in_=a[:, 128:129]), V, 1)
                if ti == 15:
                    g.wait(T, kvgp_mm[NCH - 1])
                    for hc in (20, 21, 22, 23):
                        g.op(lambda e, hc=hc:
                             e.dma_start(out=wd_bf[:, hc * 1024:(hc + 1) * 1024],
                                         in_=wdt[hc]), dWD, 16)

        def sum_prep():
            s_pre("Sigmoid")
            s.wait(V, kvgp_evac[NCH - 1])
            s.wait(S, st8["s_decay"])
            s_gprev = s.op(lambda e: e.activation(out=gcol[0][:], in_=zcol[0][:],
                                                  func=AF.Sigmoid, bias=bgbc[:]),
                           S, 1)
            v.wait(S, s_gprev)
            v_gwp = v.op(lambda e: e.tensor_tensor(out=gwcol[0][:],
                                                   in0=gcol[0][:],
                                                   in1=wrp_ext[:], op=OP.mult),
                         V, 1)
            v.wait(V, v_gwp)
            v.wait(V, kvgp_evac[NCH - 1])
            for cc in range(NCH):
                v.op(lambda e, cc=cc:
                     e.tensor_scalar(out=kv_bf[:, cc * 128 + 64: cc * 128 + 128],
                                     in0=kv_bf[:, cc * 128 + 64: cc * 128 + 128],
                                     scalar1=gcol[0][:, cc:cc + 1], scalar2=None,
                                     op0=OP.mult))
                v_ba_p[cc] = v.op(lambda e, cc=cc:
                                  e.tensor_scalar(
                                      out=kv_bf[:, cc * 128: cc * 128 + 64],
                                      in0=kv_bf[:, cc * 128: cc * 128 + 64],
                                      scalar1=gwcol[0][:, cc:cc + 1],
                                      scalar2=None, op0=OP.mult), V, 1)

        def sum_chunks():
            ds_ps = pb[2][0:M, 384:448]
            t.wait(V, v_ba_p[NCH - 1])
            t.wait(V, apply_done[15])
            t_mm = None
            for cc in range(NCH):
                sp = cc == NCH - 1
                fn = (lambda e, o=ds_ps, cc=cc, st=(cc == 0), sp=sp:
                      e.matmul(out=o,
                               lhsT=kv_bf[:, cc * 128: cc * 128 + 64],
                               rhs=kv_bf[:, cc * 128 + 64: cc * 128 + 128],
                               start=st, stop=sp))
                if sp:
                    t_mm = t.op(fn, T, 1)
                else:
                    t.op(fn)
            v.wait(T, t_mm)
            st8["v_sbf_init"] = v.op(lambda e, a=ds_ps:
                                     e.tensor_copy(out=S_bf[0][:], in_=a), V, 1)

        def qk_pass(idx):
            nt, which = idx // 2, idx % 2
            npar = idx % 2
            qk_ps = pb[3][:, npar * 512:(npar + 1) * 512]
            t.wait(V, apply_done[7])
            if idx >= 2:
                t.wait(V, qk_evac[idx - 2])
            else:
                t.wait(V, apply_done[15])
                t.wait(V, kvgp_evac[NCH - 1])
            for dc in range(NDC):
                sp = dc == NDC - 1
                fn = (lambda e, o=qk_ps, dc=dc, nt=nt, w=which,
                      st=(dc == 0), sp=sp:
                      e.matmul(out=o[0:M, :],
                               lhsT=wqk_bf[:, dc * 128 + w * 64: dc * 128 + w * 64 + 64],
                               rhs=xn[:, dc * N + nt * 512: dc * N + (nt + 1) * 512],
                               start=st, stop=sp))
                if sp:
                    t_qk[idx] = t.op(fn, T, 1)
                else:
                    t.op(fn)
            v.wait(T, t_qk[idx])
            if which == 0:
                w64b = bass.AP(w64, 0, [[128, M], [0, 4], [1, 128]])
                qo = bass.AP(qt_bf, nt * 512, [[N, M], [128, 4], [1, 128]])
                qk_evac[idx] = v.op(
                    lambda e, o=qo, a=qk_ps, b=w64b:
                    e.tensor_tensor(out=o,
                                    in0=bass.AP(a.tensor, a.offset,
                                                [[1024, M], [128, 4], [1, 128]]),
                                    in1=b, op=OP.mult), V, 1)
            else:
                qk_evac[idx] = v.op(
                    lambda e, o=k_bf[:, nt * 512:(nt + 1) * 512], a=qk_ps:
                    e.tensor_copy(out=o, in_=a[0:M, :]), V, 1)

        def d2a_chunk(cc):
            kvg_ps = (pb[2] if cc % 2 == 0 else pb[3])[
                :, ((cc // 2) % 2) * 512 + 256: ((cc // 2) % 2) * 512 + 256 + 129]
            t.wait(V, kvgo_evac[cc - 2] if cc >= 2 else st8["v_sbf_init"])
            if cc < 2:
                t.wait(V, qk_evac[7])
            t_mm = None
            for dc in range(NDC):
                sp = dc == NDC - 1
                fn = (lambda e, o=kvg_ps, dc=dc, cc=cc, st=(dc == 0), sp=sp:
                      e.matmul(out=o,
                               lhsT=xn[:, dc * N + cc * 128: dc * N + (cc + 1) * 128],
                               rhs=wkvg_bf[:, dc * 160: dc * 160 + 129],
                               start=st, stop=sp))
                if sp:
                    t_mm = t.op(fn, T, 1)
                else:
                    t.op(fn)
            v.wait(T, t_mm)
            v.op(lambda e, o=kv_bf[:, cc * 128:(cc + 1) * 128], a=kvg_ps:
                 e.tensor_copy(out=o, in_=a[:, 0:128]))
            kvgo_evac[cc] = v.op(lambda e, o=zcol[1][:, cc:cc + 1], a=kvg_ps:
                                 e.tensor_copy(out=o, in_=a[:, 128:129]), V, 1)

        def d2_gates():
            s_pre("Sigmoid")
            s.wait(V, kvgo_evac[NCH - 1])
            s_gown = s.op(lambda e: e.activation(out=gcol[1][:], in_=zcol[1][:],
                                                 func=AF.Sigmoid, bias=bgbc[:]),
                          S, 1)
            v.wait(S, s_gown)
            v.op(lambda e: e.tensor_scalar(out=gwcol[1][:], in0=gcol[1][:],
                                           scalar1=wr_col[:], scalar2=None,
                                           op0=OP.mult))
            v_gw1 = v.op(lambda e: e.tensor_scalar(out=gdcol[:], in0=gcol[1][:],
                                                   scalar1=dneg_col[:],
                                                   scalar2=None, op0=OP.mult),
                         V, 1)
            v.wait(V, v_gw1)
            v.wait(V, kvgo_evac[NCH - 1])

        def d2_ba():
            for cc in range(NCH):
                v.op(lambda e, cc=cc:
                     e.tensor_scalar(out=kv_bf[:, cc * 128 + 64: cc * 128 + 128],
                                     in0=kv_bf[:, cc * 128 + 64: cc * 128 + 128],
                                     scalar1=gcol[1][:, cc:cc + 1], scalar2=None,
                                     op0=OP.mult))
                v_ba[cc] = v.op(lambda e, cc=cc:
                                e.tensor_scalar(
                                    out=kv_bf[:, cc * 128: cc * 128 + 64],
                                    in0=kv_bf[:, cc * 128: cc * 128 + 64],
                                    scalar1=gwcol[1][:, cc:cc + 1],
                                    scalar2=None, op0=OP.mult), V, 1)

        def d2b_a(cc):
            cpar = cc % 2
            p_ps = pb[3][:, cpar * 512: cpar * 512 + 128]
            t.wait(V, v_ba[cc])
            t.wait(V, v_pt[cc - 2] if cc >= 2 else qk_evac[7])
            t_p[cc] = t.op(lambda e, o=p_ps, cc=cc:
                           e.matmul(out=o, lhsT=k_bf[:, cc * 128:(cc + 1) * 128],
                                    rhs=qt_bf[:, cc * 128:(cc + 1) * 128],
                                    start=True, stop=True), T, 1)

        def d2b_b(cc):
            cpar = cc % 2
            p_ps = pb[3][:, cpar * 512: cpar * 512 + 128]
            v.wait(T, t_p[cc])
            v_pt[cc] = v.op(lambda e, o=pt_bf[cpar], a=p_ps, cc=cc:
                            e.scalar_tensor_tensor(out=o[:], in0=a,
                                                   scalar=gdcol[:, cc:cc + 1],
                                                   in1=tril_bf[:], op0=OP.mult,
                                                   op1=OP.mult), V, 1)

        def d2b_c(cc):
            cpar = cc % 2
            sparity = cc % 2
            o_ps = pb[2][0:M, cpar * 512: cpar * 512 + 128]
            ds_ps = pb[2][0:M, cpar * 512 + 384: cpar * 512 + 448]
            t.wait(V, v_pt[cc])
            t.wait(V, v_sdone[cc - 1] if cc >= 1 else st8["v_sbf_init"])
            if cc >= 2:
                t.wait(V, v_sdone[cc - 2])
            t.op(lambda e, o=o_ps, cc=cc, cpar=cpar:
                 e.matmul(out=o, lhsT=kv_bf[:, cc * 128 + 64: cc * 128 + 128],
                          rhs=pt_bf[cpar][:], start=True, stop=False))
            t_o = t.op(lambda e, o=o_ps, cc=cc, sp=sparity:
                       e.matmul(out=o, lhsT=S_bf[sp][:],
                                rhs=qt_bf[:, cc * 128:(cc + 1) * 128],
                                start=False, stop=True), T, 1)
            t_ds2 = t.op(lambda e, o=ds_ps, cc=cc:
                         e.matmul(out=o,
                                  lhsT=kv_bf[:, cc * 128: cc * 128 + 64],
                                  rhs=kv_bf[:, cc * 128 + 64: cc * 128 + 128],
                                  start=True, stop=True), T, 1)
            v.wait(T, t_ds2)
            if cc == 0:
                v.wait(V, st8["v_sbf_init"])
            v.op(lambda e, o=outsa[0:M, cc * 128:(cc + 1) * 128], a=o_ps:
                 e.tensor_copy(out=o, in_=a))
            v_sdone[cc] = v.op(
                lambda e, o=S_bf[1 - sparity], i0=S_bf[sparity], ds=ds_ps:
                e.scalar_tensor_tensor(out=o[:], in0=i0[:],
                                       scalar=dC_col[0:M, :], in1=ds,
                                       op0=OP.mult, op1=OP.add), V, 1)

        pre = []
        for k in range(19):
            if k - 3 >= 0 and k - 3 < 16:
                pre.append(lambda ti=k - 3: lnB_c(ti))
            if k - 2 >= 0 and k - 2 < 16:
                pre.append(lambda ti=k - 2: lnB_p(ti))
            if k - 1 >= 0 and k - 1 < 16:
                pre.append(lambda ti=k - 1: (lnA_c(ti), emit_wd_loads(2)))
            if k < 16:
                pre.append(lambda ti=k: lnA_p(ti))
        pre.append(lambda: emit_wd_loads(20))
        for idx in range(8):
            pre.append(lambda idx=idx: qk_pass(idx))
        pre.append(sum_prep)
        pre.append(sum_chunks)
        for c0 in range(0, NCH, 2):
            pre.append(lambda c0=c0: [d2a_chunk(c) for c in range(c0, c0 + 2)])
        pre.append(lambda: (d2_gates(), d2_ba()))
        for k in range(NCH + 2):
            pre.append(lambda k=k: (
                d2b_a(k) if k < NCH else None,
                d2b_b(k - 1) if 0 <= k - 1 < NCH else None,
                d2b_c(k - 2) if 0 <= k - 2 < NCH else None))

        t_gu_sb = {}
        t_gu_hc = {}
        s_silu = {}
        v_h = {}
        s_fs = {}
        v_fin = {}
        st_done = {}
        xt_issue = {}

        def Gps(p):
            return pb[p][:, 0:512]

        def Ups(p):
            return pb[p][:, 512:1024]

        def load_gu(idx2):
            hc2 = idx2 % NHC
            gp2 = hc2 % 3
            for jh in range(2):
                g.op(lambda e, hc=hc2, gpar=gp2, jh=jh: e.dma_start(
                    out=gu_bf[gpar][:, jh * 1024:(jh + 1) * 1024],
                    in_=wgu[hc][:, jh * 1024:(jh + 1) * 1024]),
                    (dGUg if jh == 0 else dGU)[gp2], 16)

        def f_e1(half, hc, nh, pos):
            idx = half * NHC + hc
            gpar = hc % 3
            gsb = pos
            p = gsb % 2 if half == 0 else gsb % 4
            DEP = 2 if half == 0 else 4
            if nh == 0:
                t.wait(dGUg[gpar], 16 * (hc // 3 + 1 + (half * NHC) // 3))
            t.wait(V, apply_done[half * 4 + nh * 2 + 1])
            if half == 1 and hc == 0:
                t.wait(S, s_fs[7])
                t.wait(V, v_fin[7])
                if nh == 1:
                    t.wait(V, v_sdone[NCH - 1])
            elif half == 1 and hc == 1:
                t.wait(V, v_sdone[NCH - 1])
            elif gsb >= DEP:
                t.wait(S, s_silu[gsb - DEP])
                t.wait(V, v_h[gsb - DEP])
            t_gate = None
            for gu in range(2):
                if gu == 1 and nh == 0:
                    t.wait(dGU[gpar], 16 * (hc // 3 + 1 + (half * NHC) // 3))
                o_ps = Gps(p) if gu == 0 else Ups(p)
                for dc in range(NDC):
                    sp = dc == NDC - 1
                    fn = (lambda e, o=o_ps, gu=gu, dc=dc, gpar=gpar,
                          half=half, nh=nh, st=(dc == 0), spm=sp:
                          e.matmul(out=o,
                                   lhsT=gu_bf[gpar][:, gu * 1024 + dc * 128: gu * 1024 + (dc + 1) * 128],
                                   rhs=xn[:, dc * N + half * 1024 + nh * 512: dc * N + half * 1024 + (nh + 1) * 512],
                                   start=st, stop=spm))
                    if sp:
                        if gu == 0:
                            t_gate = t.op(fn, T, 1)
                        else:
                            t_gu_sb[gsb] = t.op(fn, T, 1)
                    else:
                        t.op(fn)
            if nh == 1:
                t_gu_hc[idx] = t_gu_sb[gsb]
                if idx + 3 < 2 * NHC:
                    g.wait(T, t_gu_hc[idx])
                    load_gu(idx + 3)
            s_pre("Silu")
            s.wait(T, t_gate)
            if gsb >= 2:
                s.wait(V, v_h[gsb - 2])
            sgp = gsb % 2
            s_silu[gsb] = s.op(lambda e, sgp=sgp, a=Gps(p):
                               e.activation(out=sg_bf[:, sgp * 512:(sgp + 1) * 512],
                                            in_=a, func=AF.Silu, scale=QS), S, 1)
            v.wait(S, s_silu[gsb])
            v.wait(T, t_gu_sb[gsb])
            v_h[gsb] = v.op(lambda e, sgp=sgp, hc=hc, nh=nh, a=Ups(p):
                            e.scalar_tensor_tensor(
                                out=h_bf[:, hc * 1024 + nh * 512: hc * 1024 + nh * 512 + 512],
                                in0=a, scalar=QS,
                                in1=sg_bf[:, sgp * 512:(sgp + 1) * 512],
                                op0=OP.mult, op1=OP.mult), V, 1)

        def f_e2(half, nc2):
            idx2 = half * 8 + nc2
            r0 = half * 1024 + nc2 * 128
            dn_ps = (pb[0] if nc2 % 2 == 0 else pb[3]) if half == 0 else pb[nc2 % 2]
            s_ps = pb[1] if half == 0 else pb[2]
            if idx2 >= 2:
                y.wait(dOT[nc2 % 2], st_done[idx2 - 2])
            else:
                y.wait(V, v_wsmall)
            y.op(lambda e, dpar=nc2 % 2, r0=r0:
                 e.dma_start(out=xtmt[dpar][:], in_=xtm[r0:r0 + 128, :]),
                 dXT[nc2 % 2], 16)
            xt_issue[idx2] = dXT[nc2 % 2].n
            gsb_last = half * 2 * NHC + 2 * NHC - 1
            if nc2 == 0:
                t.wait(V, v_h[gsb_last])
                t.wait(T, 0)
                t.wait(dWD, 16 * NHC)
                t.wait(V, v_sdone[7 if half == 0 else 15])
                t.wait(S, s_silu[gsb_last - 1])
                t.wait(V, v_h[gsb_last - 1])
            elif half == 0:
                if nc2 == 1:
                    t.wait(V, v_pt[NCH - 1])
                else:
                    t.wait(S, s_fs[idx2 - 2])
            else:
                if nc2 == 1:
                    gl = half * 2 * NHC + 2 * NHC - 1
                    t.wait(S, s_silu[gl])
                    t.wait(V, v_h[gl])
                if nc2 >= 2:
                    t.wait(S, s_fs[idx2 - 2])
            def dn_group(dh):
                for hc in range(NHC):
                    sp = hc == NHC - 1
                    fn = (lambda e, o=dn_ps, dh=dh, hc=hc, nc2=nc2,
                          st=(hc == 0), spm=sp:
                          e.matmul(out=o[:, dh * 512:(dh + 1) * 512],
                                   lhsT=h_bf[:, hc * 1024 + nc2 * 128: hc * 1024 + (nc2 + 1) * 128],
                                   rhs=wd_bf[:, hc * 1024 + dh * 512: hc * 1024 + dh * 512 + 512],
                                   start=st, stop=spm))
                    if sp:
                        t_dn[dh] = t.op(fn, T, 1)
                    else:
                        t.op(fn)

            t_dn = [None, None]
            dn_group(0)
            if idx2 >= 1:
                t.wait(V, v_fin[idx2 - 1])
            t_wo = None
            for dh in range(2):
                fn = (lambda e, o=s_ps, dh=dh, r0=r0:
                      e.matmul(out=o[:, dh * 512:(dh + 1) * 512],
                               lhsT=outsa[:, r0:r0 + 128],
                               rhs=wob_bf[:, dh * 512:(dh + 1) * 512],
                               start=True, stop=True))
                if dh == 1:
                    t_wo = t.op(fn, T, 1)
                else:
                    t.op(fn)
            dn_group(1)
            vm = [None, None]
            for dh in range(2):
                if dh == 0:
                    s_pre("Silu")
                s.wait(T, t_dn[dh])
                if idx2 >= 1 and dh == 0:
                    s.wait(V, v_fin[idx2 - 1])
                s_fs[idx2] = s.op(
                    lambda e, dh=dh, a=dn_ps:
                    e.activation(out=gf_sb[:, dh * 512:(dh + 1) * 512],
                                 in_=a[:, dh * 512:(dh + 1) * 512],
                                 func=AF.Silu, scale=QS), S, 1)
                v.wait(S, s_fs[idx2])
                if dh == 0:
                    v.wait(T, t_wo)
                    v.wait(dXT[nc2 % 2], xt_issue[idx2])
                vm[dh] = v.op(
                    lambda e, dh=dh, a=s_ps:
                    e.tensor_tensor(out=gf_sb[:, dh * 512:(dh + 1) * 512],
                                    in0=gf_sb[:, dh * 512:(dh + 1) * 512],
                                    in1=a[:, dh * 512:(dh + 1) * 512],
                                    op=OP.mult), V, 1)
                v.wait(V, vm[dh])
                v_fin[idx2] = v.op(
                    lambda e, dh=dh, dpar=nc2 % 2:
                    e.tensor_tensor(out=xtmt[dpar][:, dh * 512:(dh + 1) * 512],
                                    in0=gf_sb[:, dh * 512:(dh + 1) * 512],
                                    in1=xtmt[dpar][:, dh * 512:(dh + 1) * 512],
                                    op=OP.add), V, 1)
            y.wait(V, v_fin[idx2])
            y.op(lambda e, dpar=nc2 % 2, r0=r0:
                 e.dma_start(out=out[r0:r0 + 128, :], in_=xtmt[dpar][:]),
                 dOT[nc2 % 2], 16)
            st_done[idx2] = dOT[nc2 % 2].n

        ffn = []
        for half in range(2):
            order = [(hc, nh) for hc in range(NHC) for nh in range(2)]
            for k, (hc, nh) in enumerate(order):
                pos = half * 2 * NHC + k
                ffn.append(lambda half=half, hc=hc, nh=nh, pos=pos:
                           f_e1(half, hc, nh, pos))
            for nc2 in range(8):
                ffn.append(lambda half=half, nc2=nc2: f_e2(half, nc2))

        decay_setup()
        pre[0]()
        load_gu(0)
        pre[1]()
        pre[2]()
        load_gu(1)
        pre[3]()
        pre[4]()
        pre[5]()
        load_gu(2)
        for fn in pre[6:10]:
            fn()
        ffn[0]()
        for fn in pre[10:16]:
            fn()
        ffn[1]()
        pi = 16
        for fi, fslice in enumerate(ffn[2:]):
            if fi == 46:
                assert pi >= len(pre), (pi, len(pre))
            fslice()
            take = 2 if pi < len(pre) else 0
            for _ in range(take):
                if pi < len(pre):
                    pre[pi]()
                    pi += 1
        while pi < len(pre):
            pre[pi]()
            pi += 1

        y.wait(dOT[0], dOT[0].n)
        y.wait(dOT[1], dOT[1].n)

        block = stack.enter_context(nc.Block())

        @block.tensor
        def _(e):
            for f in ops["t"]:
                f(e)

        @block.vector
        def _(e):
            for f in ops["v"]:
                f(e)

        @block.scalar
        def _(e):
            for f in ops["s"]:
                f(e)

        @block.gpsimd
        def _(e):
            for f in ops["g"]:
                f(e)

        @block.sync
        def _(e):
            for f in ops["y"]:
                f(e)

    return nc


_CACHE = {}


def _get_nc():
    if "nc" not in _CACHE:
        nc = build_nc()
        nc.finalize()
        _CACHE["nc"] = nc
    return _CACHE["nc"]


def make_in_maps(L0_attn_x, L0_attn_Wq, L0_attn_Wk, L0_attn_Wv, L0_attn_Wo,
                 L0_attn_bo, L0_attn_Wg, L0_attn_bg, L0_attn_decay_param,
                 L0_ffn_w_gate, L0_ffn_w_up, L0_ffn_w_down):
    f32 = np.float32
    x = np.asarray(L0_attn_x, f32)
    Wq = np.asarray(L0_attn_Wq, f32)
    Wk = np.asarray(L0_attn_Wk, f32)
    Wv = np.asarray(L0_attn_Wv, f32)
    Wo = np.asarray(L0_attn_Wo, f32)
    bo = np.asarray(L0_attn_bo, f32)
    Wg = np.asarray(L0_attn_Wg, f32)
    bg = np.asarray(L0_attn_bg, f32)
    dp = np.asarray(L0_attn_decay_param, f32)
    def low_bytes(a):
        a = np.ascontiguousarray(np.asarray(a, np.int32))
        return np.ascontiguousarray(a.view(np.int8).reshape(*a.shape, 4)[..., 0])

    wg_i = low_bytes(L0_ffn_w_gate)
    wu_i = low_bytes(L0_ffn_w_up)
    wd_i = low_bytes(L0_ffn_w_down)

    wqk_c = np.concatenate([Wq, Wk], axis=1)
    wqk_t = np.ascontiguousarray(
        wqk_c.reshape(NDC, 128, 128).transpose(1, 0, 2).reshape(128, NDC * 128))
    wkvg_c = np.zeros((D, 160), f32)
    wkvg_c[:, 0:64] = Wk
    wkvg_c[:, 64:128] = Wv
    wkvg_c[:, 128:129] = Wg
    wkvg_t = np.ascontiguousarray(
        wkvg_c.reshape(NDC, 128, 160).transpose(1, 0, 2).reshape(128, NDC * 160))
    wob_c = np.ascontiguousarray(np.concatenate([Wo, bo[None, :]], axis=0))
    wgT = np.ascontiguousarray(wg_i.T)
    wuT = np.ascontiguousarray(wu_i.T)
    wg_tile = wgT.reshape(NDC, 128, NHC, 128).transpose(2, 1, 0, 3).reshape(NHC, 128, 1024)
    wu_tile = wuT.reshape(NDC, 128, NHC, 128).transpose(2, 1, 0, 3).reshape(NHC, 128, 1024)
    wgu_c = np.ascontiguousarray(
        np.concatenate([wg_tile, wu_tile], axis=2))
    wdt_c = np.ascontiguousarray(wd_i.T.reshape(NHC, 128, 1024))
    dec_c = np.ascontiguousarray(dp.reshape(1, 1))
    bgs_c = np.ascontiguousarray(bg.reshape(1, 1))

    in_maps = []
    for core in range(8):
        b, hf = core // 2, core % 2
        def tile_major(xt):
            return np.ascontiguousarray(
                xt.reshape(NDC, 128, 8, TL).transpose(2, 1, 0, 3)
                .reshape(8, 128, NDC * TL))
        xs = np.ascontiguousarray(x[b, hf * N:(hf + 1) * N, :])
        xf = tile_major(xs.T)
        if hf == 1:
            xp = tile_major(np.ascontiguousarray(x[b, 0:N, :].T))
        else:
            xp = np.zeros((8, 128, NDC * TL), f32)
        in_maps.append({
            "xfm": xf, "xpfm": xp, "xtm": xs,
            "wqk": wqk_t, "wkvg": wkvg_t, "wob": wob_c,
            "wgu": wgu_c, "wdt": wdt_c, "dec": dec_c, "bgs": bgs_c,
        })
    return in_maps


def kernel(L0_attn_x, L0_attn_Wq, L0_attn_Wk, L0_attn_Wv, L0_attn_Wo,
           L0_attn_bo, L0_attn_Wg, L0_attn_bg, L0_attn_decay_param,
           L0_ffn_w_gate, L0_ffn_w_up, L0_ffn_w_down):
    in_maps = make_in_maps(L0_attn_x, L0_attn_Wq, L0_attn_Wk, L0_attn_Wv,
                           L0_attn_Wo, L0_attn_bo, L0_attn_Wg, L0_attn_bg,
                           L0_attn_decay_param, L0_ffn_w_gate, L0_ffn_w_up,
                           L0_ffn_w_down)
    nc = _get_nc()
    res = run_bass_kernel_spmd(nc, in_maps, core_ids=list(range(8)))
    out = np.empty((4, 4096, 1024), np.float32)
    for core in range(8):
        b, hf = core // 2, core % 2
        out[b, hf * N:(hf + 1) * N, :] = res.results[core]["out"]
    return out


# version 50
# speedup vs baseline: 1.0167x; 1.0107x over previous
import numpy as np
import ml_dtypes
from contextlib import ExitStack

import concourse.bass as bass
import concourse.mybir as mybir
from concourse.bass_utils import run_bass_kernel_spmd

F32 = mybir.dt.float32
BF16 = mybir.dt.bfloat16
I32 = mybir.dt.int32
I8 = mybir.dt.int8
AF = mybir.ActivationFunctionType
OP = mybir.AluOpType

D = 1024
M = 64
H = 3072
N = 2048
CH = 128
NCH = N // CH
NDC = D // 128
NHC = H // 128
TL = 256
QS = float(np.float32(1.0 / 127.0))
EPS = 1e-5
DECAY_MIN, DECAY_MAX = 0.9, 0.999


class Sem:
    def __init__(self, handle):
        self.sem = handle
        self.n = 0


class Eng:

    def __init__(self, ops):
        self.ops = ops

    def wait(self, ctr, val):
        if val <= 0:
            return
        self.ops.append(lambda e, s=ctr.sem, v=val: e.wait_ge(s, v))

    def op(self, fn, inc=None, k=1):
        if inc is None:
            self.ops.append(fn)
            return None
        inc.n += k
        self.ops.append(lambda e, f=fn, s=inc.sem, kk=k: f(e).then_inc(s, kk))
        return inc.n

    def drain(self):
        self.ops.append(lambda e: e.drain())


class View:

    def __init__(self, base, c0):
        self.base = base
        self.c0 = c0

    def __getitem__(self, idx):
        rows, cols = idx
        if isinstance(cols, slice):
            cols = slice(self.c0 + (cols.start or 0), self.c0 + cols.stop)
        return self.base[rows, cols]


def build_nc():
    nc = bass.Bass("TRN2", target_bir_lowering=False)

    xfm = nc.dram_tensor("xfm", [8, 128, NDC * TL], F32, kind="ExternalInput").ap()
    xpfm = nc.dram_tensor("xpfm", [8, 128, NDC * TL], F32, kind="ExternalInput").ap()
    xtm = nc.dram_tensor("xtm", [N, D], F32, kind="ExternalInput").ap()
    wqk = nc.dram_tensor("wqk", [128, NDC * 128], F32, kind="ExternalInput").ap()
    wkvg = nc.dram_tensor("wkvg", [128, NDC * 160], F32, kind="ExternalInput").ap()
    wob = nc.dram_tensor("wob", [M + 1, D], F32, kind="ExternalInput").ap()
    wgu = nc.dram_tensor("wgu", [NHC, 128, 2048], I8, kind="ExternalInput").ap()
    wdt = nc.dram_tensor("wdt", [NHC, 128, 1024], I8, kind="ExternalInput").ap()
    dec = nc.dram_tensor("dec", [1, 1], F32, kind="ExternalInput").ap()
    bgs = nc.dram_tensor("bgs", [1, 1], F32, kind="ExternalInput").ap()
    out = nc.dram_tensor("out", [N, D], F32, kind="ExternalOutput").ap()

    with ExitStack() as stack:
        E_ = stack.enter_context

        xn = E_(nc.sbuf_tensor("xn", [128, NDC * N], BF16))
        wd_bf = E_(nc.sbuf_tensor("wd_bf", [128, NHC * 1024], BF16))
        h_bf = E_(nc.sbuf_tensor("h_bf", [128, NHC * 1024], BF16))
        xstage = [E_(nc.sbuf_tensor(f"xstage{i}", [128, NDC * TL], BF16)) for i in range(3)]
        xsq_f = [E_(nc.sbuf_tensor(f"xsqf{i}", [128, NDC * TL], BF16)) for i in range(2)]
        m_sb = [E_(nc.sbuf_tensor(f"msb{i}", [128, TL], F32)) for i in range(2)]
        gu_bf = [E_(nc.sbuf_tensor(f"gubf{i}", [128, 2048], BF16)) for i in range(3)]
        qt_bf = E_(nc.sbuf_tensor("qt_bf", [M, N], BF16))
        k_bf = E_(nc.sbuf_tensor("k_bf", [M, N], BF16))
        kv_bf = E_(nc.sbuf_tensor("kv_bf", [128, NCH * 128], BF16))
        outsa = E_(nc.sbuf_tensor("outsa", [M + 1, N], BF16))
        zcol = [E_(nc.sbuf_tensor(f"zcol{i}", [128, NCH], F32)) for i in range(2)]
        gcol = [E_(nc.sbuf_tensor(f"gcol{i}", [128, NCH], F32)) for i in range(2)]
        gwcol = [E_(nc.sbuf_tensor(f"gwcol{i}", [128, NCH], F32)) for i in range(2)]
        gdcol = E_(nc.sbuf_tensor("gdcol", [128, NCH], F32))
        vtmp = [E_(nc.sbuf_tensor(f"vtmp{i}", [128, TL], F32)) for i in range(2)]
        rstd = [E_(nc.sbuf_tensor(f"rstd{i}", [128, TL], F32)) for i in range(2)]
        abig = E_(nc.sbuf_tensor("abig", [128, NDC * TL // 2], F32))
        wqk_bf = E_(nc.sbuf_tensor("wqk_bf", [128, NDC * 128], BF16))
        wkvg_bf = E_(nc.sbuf_tensor("wkvg_bf", [128, NDC * 160], BF16))
        wob_bf = E_(nc.sbuf_tensor("wob_bf", [M + 1, D], BF16))
        xtmt = [E_(nc.sbuf_tensor(f"xtmt{i}", [128, D], F32)) for i in range(2)]
        gf_sb = abig
        sg_bf = E_(nc.sbuf_tensor("sgbf", [128, D], BF16))
        pt_bf = [E_(nc.sbuf_tensor(f"ptbf{i}", [128, 128], BF16)) for i in range(2)]
        tril_bf = E_(nc.sbuf_tensor("tril_bf", [128, 128], BF16))
        c1024 = E_(nc.sbuf_tensor("c1024", [128, 128], BF16))
        eps_c = E_(nc.sbuf_tensor("eps_c", [128, 1], F32))
        dec_s = E_(nc.sbuf_tensor("dec_s", [1, 1], F32))
        bgs_s = E_(nc.sbuf_tensor("bgs_s", [1, 1], F32))
        sig_s = E_(nc.sbuf_tensor("sig_s", [1, 1], F32))
        dcy_s = E_(nc.sbuf_tensor("dcy_s", [1, 1], F32))
        tscr = E_(nc.sbuf_tensor("tscr", [1, 1], F32))
        ld_s = E_(nc.sbuf_tensor("ld_s", [1, 1], F32))
        ones_r = E_(nc.sbuf_tensor("ones_r", [1, 128], F32))
        ldbc = E_(nc.sbuf_tensor("ldbc", [128, 1], F32))
        bgbc = E_(nc.sbuf_tensor("bgbc", [128, 1], F32))
        w64 = E_(nc.sbuf_tensor("w64", [M, 128], F32))
        wr_col = E_(nc.sbuf_tensor("wr_col", [128, 1], F32))
        dneg_col = E_(nc.sbuf_tensor("dneg_col", [128, 1], F32))
        dC_col = E_(nc.sbuf_tensor("dC_col", [128, 1], F32))
        S_bf = [E_(nc.sbuf_tensor(f"Sbf{i}", [M, M], BF16)) for i in range(2)]
        wrp_ext = E_(nc.sbuf_tensor("wrp_ext", [128, NCH], F32))

        xnp_w = [View(wd_bf, 20 * 1024), View(wd_bf, 22 * 1024)]

        pb = [E_(nc.psum_tensor(f"pb{i}", [128, 1024], F32)) for i in range(4)]

        mk = lambda name: Sem(E_(nc.semaphore(name)))
        dSM = mk("dSM")
        dSM2 = mk("dSM2")
        dX = [mk("dXA"), mk("dXB"), mk("dXC")]
        dX2 = [mk("dXA2"), mk("dXB2"), mk("dXC2")]
        dGU = [mk("dGU0"), mk("dGU1"), mk("dGU2")]
        dGUg = [mk("dGUg0"), mk("dGUg1"), mk("dGUg2")]
        dWD = mk("dWD")
        dXT = [mk("dXT0"), mk("dXT1")]
        dOT = [mk("dOT0"), mk("dOT1")]
        T = mk("T"); V = mk("V"); S = mk("S"); G = mk("G")

        ops = {k: [] for k in "tvsgy"}
        t = Eng(ops["t"]); v = Eng(ops["v"]); s = Eng(ops["s"])
        g = Eng(ops["g"]); y = Eng(ops["y"])

        st8 = {}

        cur_tset = ["setup"]
        dummy_last = [0]

        def s_pre(fname):
            if cur_tset[0] != fname:
                cur_tset[0] = fname
                s.wait(S, dummy_last[0])
                dummy_last[0] = s.op(lambda e, f=fname: e.activation(
                    out=tscr[:], in_=tscr[:], func=getattr(AF, f)), S, 1)

        s.op(lambda e: e.dma_start(out=abig[:], in_=wqk), dSM, 16)
        smalls_done = dSM.n
        for dst, src in [(dec_s[:], dec), (bgs_s[:], bgs)]:
            s.op(lambda e, d_=dst, s_=src: e.dma_start(out=d_, in_=s_), dSM2, 16)
        scalars_done = dSM2.n
        for dst, src in [(xtmt[0][:], wkvg[:, 0:1024]),
                         (xtmt[1][:, 0:256], wkvg[:, 1024:1280])]:
            s.op(lambda e, d_=dst, s_=src: e.dma_start(out=d_, in_=s_), dSM2, 16)
        kvg_done = dSM2.n

        g.op(lambda e: e.iota(out=wr_col[:], pattern=[[1, 1]], base=127,
                              channel_multiplier=-1,
                              allow_small_or_imprecise_dtypes=True))
        g.op(lambda e: e.iota(out=dneg_col[:], pattern=[[1, 1]], base=-1,
                              channel_multiplier=-1,
                              allow_small_or_imprecise_dtypes=True))
        g.op(lambda e: e.iota(out=w64[:], pattern=[[1, 128]], base=1,
                              channel_multiplier=0,
                              allow_small_or_imprecise_dtypes=True))
        g.op(lambda e: e.iota(out=wrp_ext[:], pattern=[[-128, NCH]], base=2047,
                              channel_multiplier=-1,
                              allow_small_or_imprecise_dtypes=True))
        g.op(lambda e: e.memset(tril_bf[:], 1.0))
        g.drain()
        g_setup = g.op(lambda e: e.affine_select(
            out=tril_bf[:], in_=tril_bf[:], pattern=[[1, 128]],
            compare_op=OP.is_ge, fill=0.0, base=0, channel_multiplier=-1), G, 1)

        v.op(lambda e: e.memset(eps_c[:], EPS))
        v.op(lambda e: e.memset(c1024[:], 1.0 / 1024.0))
        v.op(lambda e: e.memset(vtmp[0][0:1, 0:128], 1.0))
        v.op(lambda e: e.memset(ones_r[:], 1.0))
        v.op(lambda e: e.memset(outsa[M:M + 1, :], 1.0))
        v.op(lambda e: e.memset(S_bf[0][:], 0.0), V, 1)
        v_const = V.n
        v.wait(dSM, smalls_done)
        v_wqk = v.op(lambda e: e.tensor_copy(out=wqk_bf[:], in_=abig[:]), V, 1)
        v.wait(dSM2, kvg_done)
        v.op(lambda e: e.tensor_copy(out=wkvg_bf[:, 0:1024], in_=xtmt[0][:]))
        v_w1 = v.op(lambda e: e.tensor_copy(out=wkvg_bf[:, 1024:1280],
                                            in_=xtmt[1][:, 0:256]), V, 1)
        s.wait(V, v_wqk)
        s.op(lambda e: e.dma_start(out=abig[0:M + 1, :], in_=wob), dSM, 16)
        smalls2 = dSM.n
        v.wait(dSM, smalls2)
        v.op(lambda e: e.tensor_copy(out=wob_bf[:], in_=abig[0:M + 1, :]), V, 1)
        v_wsmall = V.n

        s.wait(V, v_const)
        dummy_last[0] = s.op(lambda e: e.activation(
            out=tscr[:], in_=eps_c[0:1, 0:1], func=AF.Square), S, 1)

        def decay_setup():
            s.wait(dSM2, scalars_done)
            s_sig = s.op(lambda e: e.activation(out=sig_s[:], in_=dec_s[:],
                                                func=AF.Sigmoid), S, 1)
            v.wait(S, s_sig)
            v_dcy = v.op(lambda e: e.tensor_scalar(out=dcy_s[:], in0=sig_s[:],
                                                   scalar1=DECAY_MAX - DECAY_MIN,
                                                   scalar2=DECAY_MIN, op0=OP.mult,
                                                   op1=OP.add), V, 1)
            s.wait(V, v_dcy)
            s_ld = s.op(lambda e: e.activation(out=ld_s[:], in_=dcy_s[:],
                                               func=AF.Ln), S, 1)
            t.wait(S, s_ld)
            t.wait(V, v_const)
            t.op(lambda e: e.matmul(out=pb[3][:, 448:449], lhsT=ones_r[:],
                                    rhs=ld_s[:], start=True, stop=True))
            t_bcast = t.op(lambda e: e.matmul(out=pb[3][:, 960:961],
                                              lhsT=ones_r[:],
                                              rhs=bgs_s[:], start=True, stop=True),
                           T, 1)
            v.wait(T, t_bcast)
            v.wait(G, g_setup)
            v.op(lambda e: e.tensor_copy(out=ldbc[:], in_=pb[3][:, 448:449]))
            v.op(lambda e: e.tensor_copy(out=bgbc[:], in_=pb[3][:, 960:961]))
            v.drain()
            v.op(lambda e: e.tensor_scalar(out=w64[:], in0=w64[:],
                                           scalar1=ldbc[0:M, :], scalar2=None,
                                           op0=OP.mult))
            v.op(lambda e: e.tensor_scalar(out=wr_col[:], in0=wr_col[:],
                                           scalar1=ldbc[:], scalar2=None,
                                           op0=OP.mult))
            v.op(lambda e: e.tensor_scalar(out=wrp_ext[:], in0=wrp_ext[:],
                                           scalar1=ldbc[:], scalar2=None,
                                           op0=OP.mult))
            v.op(lambda e: e.tensor_scalar(out=dneg_col[:], in0=dneg_col[:],
                                           scalar1=ldbc[:], scalar2=None,
                                           op0=OP.mult))
            v_preexp = v.op(lambda e: e.tensor_scalar(out=dC_col[:], in0=ldbc[:],
                                                      scalar1=128.0, scalar2=None,
                                                      op0=OP.mult), V, 1)
            s.wait(V, v_preexp)
            s.op(lambda e: e.activation(out=w64[:], in_=w64[:], func=AF.Exp))
            s.op(lambda e: e.activation(out=wr_col[:], in_=wr_col[:], func=AF.Exp))
            s.op(lambda e: e.activation(out=wrp_ext[:], in_=wrp_ext[:], func=AF.Exp))
            s.op(lambda e: e.activation(out=dneg_col[:], in_=dneg_col[:],
                                        func=AF.Exp))
            s_decay = s.op(lambda e: e.activation(out=dC_col[:], in_=dC_col[:],
                                                  func=AF.Exp), S, 1)
            st8["s_decay"] = s_decay
            st8["v_preexp"] = v_preexp

        apply_done = [0] * 16
        ln_mm_done = [0] * 16
        x_issue = [0] * 16
        x_issue2 = [0] * 16
        kvgp_evac = [0] * NCH
        kvgp_mm = [0] * NCH
        kvgo_evac = [0] * NCH
        qk_evac = [0] * 8
        t_qk = [0] * 8
        v_ba_p = [0] * NCH
        v_ba = [0] * NCH
        v_pt = [0] * NCH
        v_sdone = [0] * NCH
        t_p = [0] * NCH
        wd_pending = list(range(20))
        sparholder = [0]

        def emit_wd_loads(k):
            for _ in range(k):
                if not wd_pending:
                    return
                hc = wd_pending.pop(0)
                g.op(lambda e, hc=hc:
                     e.dma_start(out=wd_bf[:, hc * 1024:(hc + 1) * 1024],
                                 in_=wdt[hc]), dWD, 16)

        def m_ps_of(par):
            return pb[2][:, par * 512: par * 512 + TL]

        def s2_ps_of(par):
            return pb[3][:, par * 512: par * 512 + TL]

        sq_done = [0] * 16

        def lnA_p(ti):
            p3 = ti % 3
            own = ti < 8
            tt = ti if own else ti - 8
            if ti >= 3:
                g.wait(V, apply_done[ti - 3])
            src = (xfm if own else xpfm)[tt]
            g.op(lambda e, d_=xstage[p3][:, 0:1024], s_=src[:, 0:1024]:
                 e.dma_start(out=d_, in_=s_), dX[p3], 16)
            x_issue[ti] = dX[p3].n
            g.op(lambda e, d_=xstage[p3][:, 1024:2048], s_=src[:, 1024:2048]:
                 e.dma_start(out=d_, in_=s_), dX2[p3], 16)
            x_issue2[ti] = dX2[p3].n
            s.wait(dX[p3], x_issue[ti])
            if ti >= 2:
                s.wait(T, ln_mm_done[ti - 2])
            sq_h = [0, 0]
            for jh in range(2):
                if jh == 1:
                    s.wait(dX2[p3], x_issue2[ti])
                sq_h[jh] = s.op(
                    lambda e, jh=jh, a=xsq_f[ti % 2], b=xstage[p3]:
                    e.activation(out=a[:, jh * 1024:(jh + 1) * 1024],
                                 in_=b[:, jh * 1024:(jh + 1) * 1024],
                                 func=AF.Square), S, 1)
            sq_done[ti] = tuple(sq_h)

        def lnA_c(ti):
            par = ti % 2
            p3 = ti % 3
            m_ps, s2_ps = m_ps_of(par), s2_ps_of(par)
            t.wait(dX[p3], x_issue[ti])
            if ti >= 2:
                t.wait(V, st8[("v_var", ti - 2)])
            if ti >= 10:
                t.wait(V, kvgp_evac[2 * (ti - 2 - 8) + 1])
            for dc in range(NDC):
                if dc == NDC // 2:
                    t.wait(dX2[p3], x_issue2[ti])
                t.op(lambda e, o=m_ps, a=xstage[p3][:, dc * TL:(dc + 1) * TL],
                     st=(dc == 0), sp=(dc == NDC - 1):
                     e.matmul(out=o, lhsT=c1024[:], rhs=a, start=st, stop=sp))
            t.wait(S, sq_done[ti][0])
            for dc in range(NDC):
                if dc == NDC // 2:
                    t.wait(S, sq_done[ti][1])
                last = (lambda e, o=s2_ps,
                        a=xsq_f[ti % 2][:, dc * TL:(dc + 1) * TL],
                        st=(dc == 0), sp=(dc == NDC - 1):
                        e.matmul(out=o, lhsT=c1024[:], rhs=a, start=st, stop=sp))
                if dc == NDC - 1:
                    ln_mm_done[ti] = t.op(last, T, 1)
                else:
                    t.op(last)

        aph = {}

        def lnB_p(ti):
            par = ti % 2
            p3 = ti % 3
            own = ti < 8
            tt = ti if own else ti - 8
            m_ps, s2_ps = m_ps_of(par), s2_ps_of(par)
            s.wait(T, ln_mm_done[ti])
            if ti - 2 >= 8:
                s.wait(T, kvgp_mm[2 * (ti - 2 - 8) + 1])
            s_msq = s.op(lambda e, o=vtmp[par], a=m_ps:
                         e.activation(out=o[:], in_=a, func=AF.Square), S, 1)
            s_pre("Sqrt")
            v.wait(S, s_msq)
            v.op(lambda e, o=m_sb[par], a=m_ps:
                 e.tensor_copy(out=o[:], in_=a))
            v_var = v.op(lambda e, o=vtmp[par], a=s2_ps:
                         e.tensor_tensor(out=o[:], in0=a, in1=o[:],
                                         op=OP.subtract), V, 1)
            st8[("v_var", ti)] = v_var
            s.wait(V, v_var)
            s_std = s.op(lambda e, o=vtmp[par]:
                         e.activation(out=o[:], in_=o[:], func=AF.Sqrt,
                                      bias=eps_c[:]), S, 1)
            v.wait(S, s_std)
            v.op(lambda e, o=rstd[par], a=vtmp[par]:
                 e.reciprocal(out=o[:], in_=a[:]), V, 1)
            if not own and ti >= 10:
                v.wait(T, kvgp_mm[2 * (ti - 2 - 8) + 1])
            if ti >= 1:
                v.wait(V, apply_done[ti - 1])
            else:
                v.wait(V, v_wsmall)
            last = 0
            ap_half = [0, 0]
            for j in range(2):
                xin3 = bass.AP(xstage[p3], j * 128,
                               [[NDC * TL, 128], [TL, NDC], [1, 128]])
                ab3 = bass.AP(abig, 0, [[NDC * 128, 128], [128, NDC], [1, 128]])
                m_bc = bass.AP(m_sb[par], j * 128,
                               [[TL, 128], [0, NDC], [1, 128]])
                r_bc = bass.AP(rstd[par], j * 128,
                               [[TL, 128], [0, NDC], [1, 128]])
                if own:
                    dst3 = bass.AP(xn, tt * TL + j * 128,
                                   [[NDC * N, 128], [N, NDC], [1, 128]])
                else:
                    c0 = (20 + 2 * par) * 1024 + j * 128
                    dst3 = bass.AP(wd_bf, c0,
                                   [[NHC * 1024, 128], [TL, NDC], [1, 128]])
                if j == 1:
                    v.wait(V, last)
                v_sub = v.op(lambda e, o=ab3, a=xin3, b=m_bc:
                             e.tensor_tensor(out=o, in0=a, in1=b,
                                             op=OP.subtract), V, 1)
                v.wait(V, v_sub)
                last = v.op(lambda e, o=dst3, a=ab3, b=r_bc:
                            e.tensor_tensor(out=o, in0=a, in1=b, op=OP.mult),
                            V, 1)
                ap_half[j] = last
            aph[ti] = ap_half
            apply_done[ti] = last

        def lnB_c(ti):
            par = ti % 2
            own = ti < 8
            tt = ti if own else ti - 8
            ap_half = aph[ti]
            if not own:
                for j in range(2):
                    cc = 2 * tt + j
                    kvg_ps = (pb[2] if j == 0 else pb[3])[
                        :, par * 512 + 256: par * 512 + 256 + 129]
                    t.wait(V, ap_half[j])
                    t.wait(V, v_wsmall)
                    if cc >= 2:
                        t.wait(V, kvgp_evac[cc - 2])
                    t_mm = None
                    for dc in range(NDC):
                        sp = dc == NDC - 1
                        fn = (lambda e, o=kvg_ps, dc=dc, par=par, j=j,
                              st=(dc == 0), sp=sp:
                              e.matmul(out=o,
                                       lhsT=xnp_w[par][:, dc * TL + j * 128: dc * TL + j * 128 + 128],
                                       rhs=wkvg_bf[:, dc * 160: dc * 160 + 129],
                                       start=st, stop=sp))
                        if sp:
                            t_mm = t.op(fn, T, 1)
                        else:
                            t.op(fn)
                    kvgp_mm[cc] = t_mm
                    v.wait(T, t_mm)
                    v.op(lambda e, o=kv_bf[:, cc * 128:(cc + 1) * 128],
                         a=kvg_ps: e.tensor_copy(out=o, in_=a[:, 0:128]))
                    kvgp_evac[cc] = v.op(
                        lambda e, o=zcol[0][:, cc:cc + 1], a=kvg_ps:
                        e.tensor_copy(out=o, in_=a[:, 128:129]), V, 1)
                if ti == 15:
                    g.wait(T, kvgp_mm[NCH - 1])
                    for hc in (20, 21, 22, 23):
                        g.op(lambda e, hc=hc:
                             e.dma_start(out=wd_bf[:, hc * 1024:(hc + 1) * 1024],
                                         in_=wdt[hc]), dWD, 16)

        def sum_prep():
            s_pre("Sigmoid")
            s.wait(V, kvgp_evac[NCH - 1])
            s.wait(S, st8["s_decay"])
            s_gprev = s.op(lambda e: e.activation(out=gcol[0][:], in_=zcol[0][:],
                                                  func=AF.Sigmoid, bias=bgbc[:]),
                           S, 1)
            v.wait(S, s_gprev)
            v_gwp = v.op(lambda e: e.tensor_tensor(out=gwcol[0][:],
                                                   in0=gcol[0][:],
                                                   in1=wrp_ext[:], op=OP.mult),
                         V, 1)
            v.wait(V, v_gwp)
            v.wait(V, kvgp_evac[NCH - 1])
            for cc in range(NCH):
                v.op(lambda e, cc=cc:
                     e.tensor_scalar(out=kv_bf[:, cc * 128 + 64: cc * 128 + 128],
                                     in0=kv_bf[:, cc * 128 + 64: cc * 128 + 128],
                                     scalar1=gcol[0][:, cc:cc + 1], scalar2=None,
                                     op0=OP.mult))
                v_ba_p[cc] = v.op(lambda e, cc=cc:
                                  e.tensor_scalar(
                                      out=kv_bf[:, cc * 128: cc * 128 + 64],
                                      in0=kv_bf[:, cc * 128: cc * 128 + 64],
                                      scalar1=gwcol[0][:, cc:cc + 1],
                                      scalar2=None, op0=OP.mult), V, 1)

        def sum_chunks():
            ds_ps = pb[2][0:M, 384:448]
            t.wait(V, v_ba_p[NCH - 1])
            t.wait(V, apply_done[15])
            t_mm = None
            for cc in range(NCH):
                sp = cc == NCH - 1
                fn = (lambda e, o=ds_ps, cc=cc, st=(cc == 0), sp=sp:
                      e.matmul(out=o,
                               lhsT=kv_bf[:, cc * 128: cc * 128 + 64],
                               rhs=kv_bf[:, cc * 128 + 64: cc * 128 + 128],
                               start=st, stop=sp))
                if sp:
                    t_mm = t.op(fn, T, 1)
                else:
                    t.op(fn)
            v.wait(T, t_mm)
            st8["v_sbf_init"] = v.op(lambda e, a=ds_ps:
                                     e.tensor_copy(out=S_bf[0][:], in_=a), V, 1)

        def qk_pass(idx):
            nt, which = idx // 2, idx % 2
            npar = idx % 2
            qk_ps = pb[3][:, npar * 512:(npar + 1) * 512]
            t.wait(V, apply_done[7])
            if idx >= 2:
                t.wait(V, qk_evac[idx - 2])
            else:
                t.wait(V, apply_done[15])
                t.wait(V, kvgp_evac[NCH - 1])
            for dc in range(NDC):
                sp = dc == NDC - 1
                fn = (lambda e, o=qk_ps, dc=dc, nt=nt, w=which,
                      st=(dc == 0), sp=sp:
                      e.matmul(out=o[0:M, :],
                               lhsT=wqk_bf[:, dc * 128 + w * 64: dc * 128 + w * 64 + 64],
                               rhs=xn[:, dc * N + nt * 512: dc * N + (nt + 1) * 512],
                               start=st, stop=sp))
                if sp:
                    t_qk[idx] = t.op(fn, T, 1)
                else:
                    t.op(fn)
            v.wait(T, t_qk[idx])
            if which == 0:
                w64b = bass.AP(w64, 0, [[128, M], [0, 4], [1, 128]])
                qo = bass.AP(qt_bf, nt * 512, [[N, M], [128, 4], [1, 128]])
                qk_evac[idx] = v.op(
                    lambda e, o=qo, a=qk_ps, b=w64b:
                    e.tensor_tensor(out=o,
                                    in0=bass.AP(a.tensor, a.offset,
                                                [[1024, M], [128, 4], [1, 128]]),
                                    in1=b, op=OP.mult), V, 1)
            else:
                qk_evac[idx] = v.op(
                    lambda e, o=k_bf[:, nt * 512:(nt + 1) * 512], a=qk_ps:
                    e.tensor_copy(out=o, in_=a[0:M, :]), V, 1)

        def d2a_chunk(cc):
            kvg_ps = (pb[2] if cc % 2 == 0 else pb[3])[
                :, ((cc // 2) % 2) * 512 + 256: ((cc // 2) % 2) * 512 + 256 + 129]
            t.wait(V, kvgo_evac[cc - 2] if cc >= 2 else st8["v_sbf_init"])
            if cc < 2:
                t.wait(V, qk_evac[7])
            t_mm = None
            for dc in range(NDC):
                sp = dc == NDC - 1
                fn = (lambda e, o=kvg_ps, dc=dc, cc=cc, st=(dc == 0), sp=sp:
                      e.matmul(out=o,
                               lhsT=xn[:, dc * N + cc * 128: dc * N + (cc + 1) * 128],
                               rhs=wkvg_bf[:, dc * 160: dc * 160 + 129],
                               start=st, stop=sp))
                if sp:
                    t_mm = t.op(fn, T, 1)
                else:
                    t.op(fn)
            v.wait(T, t_mm)
            v.op(lambda e, o=kv_bf[:, cc * 128:(cc + 1) * 128], a=kvg_ps:
                 e.tensor_copy(out=o, in_=a[:, 0:128]))
            kvgo_evac[cc] = v.op(lambda e, o=zcol[1][:, cc:cc + 1], a=kvg_ps:
                                 e.tensor_copy(out=o, in_=a[:, 128:129]), V, 1)

        def d2_gates():
            s_pre("Sigmoid")
            s.wait(V, kvgo_evac[NCH - 1])
            s_gown = s.op(lambda e: e.activation(out=gcol[1][:], in_=zcol[1][:],
                                                 func=AF.Sigmoid, bias=bgbc[:]),
                          S, 1)
            v.wait(S, s_gown)
            v.op(lambda e: e.tensor_scalar(out=gwcol[1][:], in0=gcol[1][:],
                                           scalar1=wr_col[:], scalar2=None,
                                           op0=OP.mult))
            v_gw1 = v.op(lambda e: e.tensor_scalar(out=gdcol[:], in0=gcol[1][:],
                                                   scalar1=dneg_col[:],
                                                   scalar2=None, op0=OP.mult),
                         V, 1)
            v.wait(V, v_gw1)
            v.wait(V, kvgo_evac[NCH - 1])

        def d2_ba():
            for cc in range(NCH):
                v.op(lambda e, cc=cc:
                     e.tensor_scalar(out=kv_bf[:, cc * 128 + 64: cc * 128 + 128],
                                     in0=kv_bf[:, cc * 128 + 64: cc * 128 + 128],
                                     scalar1=gcol[1][:, cc:cc + 1], scalar2=None,
                                     op0=OP.mult))
                v_ba[cc] = v.op(lambda e, cc=cc:
                                e.tensor_scalar(
                                    out=kv_bf[:, cc * 128: cc * 128 + 64],
                                    in0=kv_bf[:, cc * 128: cc * 128 + 64],
                                    scalar1=gwcol[1][:, cc:cc + 1],
                                    scalar2=None, op0=OP.mult), V, 1)

        def d2b_a(cc):
            cpar = cc % 2
            p_ps = pb[3][:, cpar * 512: cpar * 512 + 128]
            t.wait(V, v_ba[cc])
            t.wait(V, v_pt[cc - 2] if cc >= 2 else qk_evac[7])
            t_p[cc] = t.op(lambda e, o=p_ps, cc=cc:
                           e.matmul(out=o, lhsT=k_bf[:, cc * 128:(cc + 1) * 128],
                                    rhs=qt_bf[:, cc * 128:(cc + 1) * 128],
                                    start=True, stop=True), T, 1)

        def d2b_b(cc):
            cpar = cc % 2
            p_ps = pb[3][:, cpar * 512: cpar * 512 + 128]
            v.wait(T, t_p[cc])
            v_pt[cc] = v.op(lambda e, o=pt_bf[cpar], a=p_ps, cc=cc:
                            e.scalar_tensor_tensor(out=o[:], in0=a,
                                                   scalar=gdcol[:, cc:cc + 1],
                                                   in1=tril_bf[:], op0=OP.mult,
                                                   op1=OP.mult), V, 1)

        def d2b_c(cc):
            cpar = cc % 2
            sparity = cc % 2
            o_ps = pb[2][0:M, cpar * 512: cpar * 512 + 128]
            ds_ps = pb[2][0:M, cpar * 512 + 384: cpar * 512 + 448]
            t.wait(V, v_pt[cc])
            t.wait(V, v_sdone[cc - 1] if cc >= 1 else st8["v_sbf_init"])
            if cc >= 2:
                t.wait(V, v_sdone[cc - 2])
            t.op(lambda e, o=o_ps, cc=cc, cpar=cpar:
                 e.matmul(out=o, lhsT=kv_bf[:, cc * 128 + 64: cc * 128 + 128],
                          rhs=pt_bf[cpar][:], start=True, stop=False))
            t_o = t.op(lambda e, o=o_ps, cc=cc, sp=sparity:
                       e.matmul(out=o, lhsT=S_bf[sp][:],
                                rhs=qt_bf[:, cc * 128:(cc + 1) * 128],
                                start=False, stop=True), T, 1)
            t_ds2 = t.op(lambda e, o=ds_ps, cc=cc:
                         e.matmul(out=o,
                                  lhsT=kv_bf[:, cc * 128: cc * 128 + 64],
                                  rhs=kv_bf[:, cc * 128 + 64: cc * 128 + 128],
                                  start=True, stop=True), T, 1)
            v.wait(T, t_ds2)
            if cc == 0:
                v.wait(V, st8["v_sbf_init"])
            v.op(lambda e, o=outsa[0:M, cc * 128:(cc + 1) * 128], a=o_ps:
                 e.tensor_copy(out=o, in_=a))
            v_sdone[cc] = v.op(
                lambda e, o=S_bf[1 - sparity], i0=S_bf[sparity], ds=ds_ps:
                e.scalar_tensor_tensor(out=o[:], in0=i0[:],
                                       scalar=dC_col[0:M, :], in1=ds,
                                       op0=OP.mult, op1=OP.add), V, 1)

        pre = []
        for k in range(19):
            if k - 3 >= 0 and k - 3 < 16:
                pre.append(lambda ti=k - 3: lnB_c(ti))
            if k - 2 >= 0 and k - 2 < 16:
                pre.append(lambda ti=k - 2: lnB_p(ti))
            if k - 1 >= 0 and k - 1 < 16:
                pre.append(lambda ti=k - 1: (lnA_c(ti), emit_wd_loads(2)))
            if k < 16:
                pre.append(lambda ti=k: lnA_p(ti))
        pre.append(lambda: emit_wd_loads(20))
        for idx in range(8):
            pre.append(lambda idx=idx: qk_pass(idx))
        pre.append(sum_prep)
        pre.append(sum_chunks)
        for c0 in range(0, NCH, 2):
            pre.append(lambda c0=c0: [d2a_chunk(c) for c in range(c0, c0 + 2)])
        pre.append(lambda: (d2_gates(), d2_ba()))
        for k in range(NCH + 2):
            pre.append(lambda k=k: (
                d2b_a(k) if k < NCH else None,
                d2b_b(k - 1) if 0 <= k - 1 < NCH else None,
                d2b_c(k - 2) if 0 <= k - 2 < NCH else None))

        t_gu_sb = {}
        t_gu_hc = {}
        s_silu = {}
        v_h = {}
        s_fs = {}
        v_fin = {}
        st_done = {}
        xt_issue = {}

        def Gps(p):
            return pb[p][:, 0:512]

        def Ups(p):
            return pb[p][:, 512:1024]

        def load_gu(idx2):
            hc2 = idx2 % NHC
            gp2 = hc2 % 3
            for jh in range(2):
                g.op(lambda e, hc=hc2, gpar=gp2, jh=jh: e.dma_start(
                    out=gu_bf[gpar][:, jh * 1024:(jh + 1) * 1024],
                    in_=wgu[hc][:, jh * 1024:(jh + 1) * 1024]),
                    (dGUg if jh == 0 else dGU)[gp2], 16)

        def f_e1(half, hc, nh, pos):
            idx = half * NHC + hc
            gpar = hc % 3
            gsb = pos
            p = gsb % 2 if half == 0 else gsb % 4
            DEP = 2 if half == 0 else 4
            if nh == 0:
                t.wait(dGUg[gpar], 16 * (hc // 3 + 1 + (half * NHC) // 3))
            t.wait(V, apply_done[half * 4 + nh * 2 + 1])
            if half == 1 and hc == 0:
                t.wait(S, s_fs[7])
                t.wait(V, v_fin[7])
                if nh == 1:
                    t.wait(V, v_sdone[NCH - 1])
            elif half == 1 and hc == 1:
                t.wait(V, v_sdone[NCH - 1])
            elif gsb >= DEP:
                t.wait(S, s_silu[gsb - DEP])
                t.wait(V, v_h[gsb - DEP])
            t_gate = None
            for gu in range(2):
                if gu == 1 and nh == 0:
                    t.wait(dGU[gpar], 16 * (hc // 3 + 1 + (half * NHC) // 3))
                o_ps = Gps(p) if gu == 0 else Ups(p)
                for dc in range(NDC):
                    sp = dc == NDC - 1
                    fn = (lambda e, o=o_ps, gu=gu, dc=dc, gpar=gpar,
                          half=half, nh=nh, st=(dc == 0), spm=sp:
                          e.matmul(out=o,
                                   lhsT=gu_bf[gpar][:, gu * 1024 + dc * 128: gu * 1024 + (dc + 1) * 128],
                                   rhs=xn[:, dc * N + half * 1024 + nh * 512: dc * N + half * 1024 + (nh + 1) * 512],
                                   start=st, stop=spm))
                    if sp:
                        if gu == 0:
                            t_gate = t.op(fn, T, 1)
                        else:
                            t_gu_sb[gsb] = t.op(fn, T, 1)
                    else:
                        t.op(fn)
            if nh == 1:
                t_gu_hc[idx] = t_gu_sb[gsb]
                if idx + 3 < 2 * NHC:
                    g.wait(T, t_gu_hc[idx])
                    load_gu(idx + 3)
            s_pre("Silu")
            s.wait(T, t_gate)
            if gsb >= 2:
                s.wait(V, v_h[gsb - 2])
            sgp = gsb % 2
            s_silu[gsb] = s.op(lambda e, sgp=sgp, a=Gps(p):
                               e.activation(out=sg_bf[:, sgp * 512:(sgp + 1) * 512],
                                            in_=a, func=AF.Silu, scale=QS), S, 1)
            v.wait(S, s_silu[gsb])
            v.wait(T, t_gu_sb[gsb])
            v_h[gsb] = v.op(lambda e, sgp=sgp, hc=hc, nh=nh, a=Ups(p):
                            e.scalar_tensor_tensor(
                                out=h_bf[:, hc * 1024 + nh * 512: hc * 1024 + nh * 512 + 512],
                                in0=a, scalar=QS,
                                in1=sg_bf[:, sgp * 512:(sgp + 1) * 512],
                                op0=OP.mult, op1=OP.mult), V, 1)

        def f_e2(half, nc2):
            idx2 = half * 8 + nc2
            r0 = half * 1024 + nc2 * 128
            dn_ps = (pb[0] if nc2 % 2 == 0 else pb[3]) if half == 0 else pb[nc2 % 2]
            s_ps = pb[1] if half == 0 else pb[2]
            if idx2 >= 2:
                y.wait(dOT[nc2 % 2], st_done[idx2 - 2])
            else:
                y.wait(V, v_wsmall)
            y.op(lambda e, dpar=nc2 % 2, r0=r0:
                 e.dma_start(out=xtmt[dpar][:], in_=xtm[r0:r0 + 128, :]),
                 dXT[nc2 % 2], 16)
            xt_issue[idx2] = dXT[nc2 % 2].n
            gsb_last = half * 2 * NHC + 2 * NHC - 1
            if nc2 == 0:
                t.wait(V, v_h[gsb_last])
                t.wait(T, 0)
                t.wait(dWD, 16 * NHC)
                t.wait(V, v_sdone[7 if half == 0 else 15])
                t.wait(S, s_silu[gsb_last - 1])
                t.wait(V, v_h[gsb_last - 1])
            elif half == 0:
                if nc2 == 1:
                    t.wait(V, v_pt[NCH - 1])
                else:
                    t.wait(S, s_fs[idx2 - 2])
            else:
                if nc2 == 1:
                    gl = half * 2 * NHC + 2 * NHC - 1
                    t.wait(S, s_silu[gl])
                    t.wait(V, v_h[gl])
                if nc2 >= 2:
                    t.wait(S, s_fs[idx2 - 2])
            def dn_group(dh):
                for hc in range(NHC):
                    sp = hc == NHC - 1
                    fn = (lambda e, o=dn_ps, dh=dh, hc=hc, nc2=nc2,
                          st=(hc == 0), spm=sp:
                          e.matmul(out=o[:, dh * 512:(dh + 1) * 512],
                                   lhsT=h_bf[:, hc * 1024 + nc2 * 128: hc * 1024 + (nc2 + 1) * 128],
                                   rhs=wd_bf[:, hc * 1024 + dh * 512: hc * 1024 + dh * 512 + 512],
                                   start=st, stop=spm))
                    if sp:
                        t_dn[dh] = t.op(fn, T, 1)
                    else:
                        t.op(fn)

            t_dn = [None, None]
            dn_group(0)
            if idx2 >= 1:
                t.wait(V, v_fin[idx2 - 1])
            t_wo = None
            for dh in range(2):
                fn = (lambda e, o=s_ps, dh=dh, r0=r0:
                      e.matmul(out=o[:, dh * 512:(dh + 1) * 512],
                               lhsT=outsa[:, r0:r0 + 128],
                               rhs=wob_bf[:, dh * 512:(dh + 1) * 512],
                               start=True, stop=True))
                if dh == 1:
                    t_wo = t.op(fn, T, 1)
                else:
                    t.op(fn)
            dn_group(1)
            vm = [None, None]
            for dh in range(2):
                if dh == 0:
                    s_pre("Silu")
                s.wait(T, t_dn[dh])
                if idx2 >= 1 and dh == 0:
                    s.wait(V, v_fin[idx2 - 1])
                s_fs[idx2] = s.op(
                    lambda e, dh=dh, a=dn_ps:
                    e.activation(out=gf_sb[:, dh * 512:(dh + 1) * 512],
                                 in_=a[:, dh * 512:(dh + 1) * 512],
                                 func=AF.Silu, scale=QS), S, 1)
                v.wait(S, s_fs[idx2])
                if dh == 0:
                    v.wait(T, t_wo)
                    v.wait(dXT[nc2 % 2], xt_issue[idx2])
                vm[dh] = v.op(
                    lambda e, dh=dh, a=s_ps:
                    e.tensor_tensor(out=gf_sb[:, dh * 512:(dh + 1) * 512],
                                    in0=gf_sb[:, dh * 512:(dh + 1) * 512],
                                    in1=a[:, dh * 512:(dh + 1) * 512],
                                    op=OP.mult), V, 1)
                v.wait(V, vm[dh])
                v_fin[idx2] = v.op(
                    lambda e, dh=dh, dpar=nc2 % 2:
                    e.tensor_tensor(out=xtmt[dpar][:, dh * 512:(dh + 1) * 512],
                                    in0=gf_sb[:, dh * 512:(dh + 1) * 512],
                                    in1=xtmt[dpar][:, dh * 512:(dh + 1) * 512],
                                    op=OP.add), V, 1)
            y.wait(V, v_fin[idx2])
            y.op(lambda e, dpar=nc2 % 2, r0=r0:
                 e.dma_start(out=out[r0:r0 + 128, :], in_=xtmt[dpar][:]),
                 dOT[nc2 % 2], 16)
            st_done[idx2] = dOT[nc2 % 2].n

        ffn = []
        for half in range(2):
            order = [(hc, nh) for hc in range(NHC) for nh in range(2)]
            for k, (hc, nh) in enumerate(order):
                pos = half * 2 * NHC + k
                ffn.append(lambda half=half, hc=hc, nh=nh, pos=pos:
                           f_e1(half, hc, nh, pos))
            for nc2 in range(8):
                ffn.append(lambda half=half, nc2=nc2: f_e2(half, nc2))

        decay_setup()
        pre[0]()
        load_gu(0)
        pre[1]()
        pre[2]()
        load_gu(1)
        pre[3]()
        pre[4]()
        pre[5]()
        load_gu(2)
        for fn in pre[6:10]:
            fn()
        ffn[0]()
        for fn in pre[10:16]:
            fn()
        ffn[1]()
        pi = 16
        for fi, fslice in enumerate(ffn[2:]):
            if fi == 46:
                assert pi >= len(pre), (pi, len(pre))
            fslice()
            take = 2 if pi < len(pre) else 0
            for _ in range(take):
                if pi < len(pre):
                    pre[pi]()
                    pi += 1
        while pi < len(pre):
            pre[pi]()
            pi += 1

        y.wait(dOT[0], dOT[0].n)
        y.wait(dOT[1], dOT[1].n)

        block = stack.enter_context(nc.Block())

        @block.tensor
        def _(e):
            for f in ops["t"]:
                f(e)

        @block.vector
        def _(e):
            for f in ops["v"]:
                f(e)

        @block.scalar
        def _(e):
            for f in ops["s"]:
                f(e)

        @block.gpsimd
        def _(e):
            for f in ops["g"]:
                f(e)

        @block.sync
        def _(e):
            for f in ops["y"]:
                f(e)

    return nc


_CACHE = {}


def _get_nc():
    if "nc" not in _CACHE:
        nc = build_nc()
        nc.finalize()
        _CACHE["nc"] = nc
    return _CACHE["nc"]


def make_in_maps(L0_attn_x, L0_attn_Wq, L0_attn_Wk, L0_attn_Wv, L0_attn_Wo,
                 L0_attn_bo, L0_attn_Wg, L0_attn_bg, L0_attn_decay_param,
                 L0_ffn_w_gate, L0_ffn_w_up, L0_ffn_w_down):
    f32 = np.float32
    x = np.asarray(L0_attn_x, f32)
    Wq = np.asarray(L0_attn_Wq, f32)
    Wk = np.asarray(L0_attn_Wk, f32)
    Wv = np.asarray(L0_attn_Wv, f32)
    Wo = np.asarray(L0_attn_Wo, f32)
    bo = np.asarray(L0_attn_bo, f32)
    Wg = np.asarray(L0_attn_Wg, f32)
    bg = np.asarray(L0_attn_bg, f32)
    dp = np.asarray(L0_attn_decay_param, f32)
    def low_bytes(a):
        a = np.ascontiguousarray(np.asarray(a, np.int32))
        return np.ascontiguousarray(a.view(np.int8).reshape(*a.shape, 4)[..., 0])

    wg_i = low_bytes(L0_ffn_w_gate)
    wu_i = low_bytes(L0_ffn_w_up)
    wd_i = low_bytes(L0_ffn_w_down)

    wqk_c = np.concatenate([Wq, Wk], axis=1)
    wqk_t = np.ascontiguousarray(
        wqk_c.reshape(NDC, 128, 128).transpose(1, 0, 2).reshape(128, NDC * 128))
    wkvg_c = np.zeros((D, 160), f32)
    wkvg_c[:, 0:64] = Wk
    wkvg_c[:, 64:128] = Wv
    wkvg_c[:, 128:129] = Wg
    wkvg_t = np.ascontiguousarray(
        wkvg_c.reshape(NDC, 128, 160).transpose(1, 0, 2).reshape(128, NDC * 160))
    wob_c = np.ascontiguousarray(np.concatenate([Wo, bo[None, :]], axis=0))
    wgT = np.ascontiguousarray(wg_i.T)
    wuT = np.ascontiguousarray(wu_i.T)
    wg_tile = wgT.reshape(NDC, 128, NHC, 128).transpose(2, 1, 0, 3).reshape(NHC, 128, 1024)
    wu_tile = wuT.reshape(NDC, 128, NHC, 128).transpose(2, 1, 0, 3).reshape(NHC, 128, 1024)
    wgu_c = np.ascontiguousarray(
        np.concatenate([wg_tile, wu_tile], axis=2))
    wdt_c = np.ascontiguousarray(wd_i.T.reshape(NHC, 128, 1024))
    dec_c = np.ascontiguousarray(dp.reshape(1, 1))
    bgs_c = np.ascontiguousarray(bg.reshape(1, 1))

    in_maps = []
    for core in range(8):
        b, hf = core // 2, core % 2
        def tile_major(xt):
            return np.ascontiguousarray(
                xt.reshape(NDC, 128, 8, TL).transpose(2, 1, 0, 3)
                .reshape(8, 128, NDC * TL))
        xs = np.ascontiguousarray(x[b, hf * N:(hf + 1) * N, :])
        xf = tile_major(xs.T)
        if hf == 1:
            xp = tile_major(np.ascontiguousarray(x[b, 0:N, :].T))
        else:
            xp = np.zeros((8, 128, NDC * TL), f32)
        in_maps.append({
            "xfm": xf, "xpfm": xp, "xtm": xs,
            "wqk": wqk_t, "wkvg": wkvg_t, "wob": wob_c,
            "wgu": wgu_c, "wdt": wdt_c, "dec": dec_c, "bgs": bgs_c,
        })
    return in_maps


def kernel(L0_attn_x, L0_attn_Wq, L0_attn_Wk, L0_attn_Wv, L0_attn_Wo,
           L0_attn_bo, L0_attn_Wg, L0_attn_bg, L0_attn_decay_param,
           L0_ffn_w_gate, L0_ffn_w_up, L0_ffn_w_down):
    in_maps = make_in_maps(L0_attn_x, L0_attn_Wq, L0_attn_Wk, L0_attn_Wv,
                           L0_attn_Wo, L0_attn_bo, L0_attn_Wg, L0_attn_bg,
                           L0_attn_decay_param, L0_ffn_w_gate, L0_ffn_w_up,
                           L0_ffn_w_down)
    nc = _get_nc()
    res = run_bass_kernel_spmd(nc, in_maps, core_ids=list(range(8)))
    out = np.empty((4, 4096, 1024), np.float32)
    for core in range(8):
        b, hf = core // 2, core % 2
        out[b, hf * N:(hf + 1) * N, :] = res.results[core]["out"]
    return out
